# version 13
# baseline (speedup 1.0000x reference)
import math
from contextlib import ExitStack

import numpy as np
import ml_dtypes

import concourse.bass as bass
import concourse.mybir as mybir
from concourse.bass_utils import run_bass_kernel_spmd

F32 = mybir.dt.float32
BF16 = mybir.dt.bfloat16
I32 = mybir.dt.int32
AF = mybir.ActivationFunctionType
ALU = mybir.AluOpType
AX = mybir.AxisListType
BF = ml_dtypes.bfloat16

S = 2048
D = 4096
H = 32
HD = 128
NT = S // 128
NCH = D // 128
EPS = 1e-5
ISQ = 1.0 / math.sqrt(HD)


def _pi_perm():
    pi = []
    for q in range(4):
        pi += list(range(16 * q, 16 * q + 16)) + list(range(64 + 16 * q, 64 + 16 * q + 16))
    return np.array(pi, dtype=np.int64)

PI = _pi_perm()
SHUF_MASK = list(range(16, 32)) + list(range(0, 16))


def _rope_tables():
    inv_freq = 1.0 / 10000.0 ** (np.arange(0, HD, 2, dtype=np.float64) / HD)
    pos = np.arange(S, dtype=np.float64)
    ang = pos[None, :] * inv_freq[:, None]
    cosv, sinv = np.cos(ang), np.sin(ang)
    cos_t = np.zeros((128, S), dtype=np.float64)
    sins_t = np.zeros((128, S), dtype=np.float64)
    for p in range(128):
        d = int(PI[p])
        cos_t[p] = cosv[d % 64]
        sins_t[p] = (-1.0 if d < 64 else 1.0) * sinv[d % 64]
    return cos_t.astype(BF), sins_t.astype(BF)


def _tri01():
    k = np.arange(128)[:, None]
    q = np.arange(128)[None, :]
    return (q >= k).astype(BF)



class Qw:
    def __init__(self, stack, nc, eng, name, dma=False, fence=False):
        self.e = eng
        self.sem = stack.enter_context(nc.semaphore(name))
        self.n = 0
        self.inc = 16 if dma else 1
        self.fence = fence
        self.waited = {}

    def mark(self, ins):
        if self.fence:
            ins = ins._wait_ge(self.sem, self.n)
        ins.then_inc(self.sem, self.inc)
        self.n += self.inc
        return (self.sem, self.n)

    def wait(self, *tokens):
        for t in tokens:
            if t is None:
                continue
            sem, v = t
            k = id(sem)
            if self.waited.get(k, -1) >= v:
                continue
            self.e.wait_ge(sem, v)
            self.waited[k] = v

    def tok(self):
        return (self.sem, self.n)


class Box:
    pass


_NAME_CTR = [0]


def _uname(name):
    _NAME_CTR[0] += 1
    return f"{name}_{_NAME_CTR[0]}"


def _subctx(ctx, st):
    c = Box()
    c.__dict__.update(ctx.__dict__)
    c.sb = lambda name, shape, dt: st.enter_context(ctx.nc.sbuf_tensor(_uname(name), shape, dt))
    return c


def _ln_smalls(ctx, tag):
    t = Box()
    for f in ("sum", "ssq", "nm", "msq", "var", "rstd"):
        setattr(t, f, ctx.sb(f"ln_{f}_{tag}", [128, 1], F32))
    return t


def emit_ln(ctx, src_ap, trash_ap, tm, outs, dep, out_deps=(), trash_dep=()):
    V, A = ctx.V, ctx.A
    nc = ctx.nc
    n = src_ap.shape[-1]
    A.wait(*dep)
    A.wait(*trash_dep)
    tA = A.mark(nc.scalar.activation(out=trash_ap, in_=src_ap, func=AF.Square,
                                     accum_out=tm.ssq[:, 0:1]))
    V.wait(*dep)
    t1 = V.mark(nc.vector.tensor_reduce(out=tm.sum[:, 0:1], in_=src_ap, axis=AX.X, op=ALU.add))
    V.wait(t1)
    t2 = V.mark(nc.vector.tensor_scalar_mul(out=tm.nm[:, 0:1], in0=tm.sum[:, 0:1], scalar1=-1.0 / n))
    V.wait(t2)
    t3 = V.mark(nc.vector.tensor_tensor(out=tm.msq[:, 0:1], in0=tm.nm[:, 0:1], in1=tm.nm[:, 0:1], op=ALU.mult))
    V.wait(tA, t3)
    t4 = V.mark(nc.vector.scalar_tensor_tensor(out=tm.var[:, 0:1], in0=tm.ssq[:, 0:1],
                                               scalar=1.0 / n, in1=tm.msq[:, 0:1],
                                               op0=ALU.mult, op1=ALU.subtract))
    A.wait(t4)
    t5 = A.mark(nc.scalar.activation(out=tm.var[:, 0:1], in_=tm.var[:, 0:1], func=AF.Ln,
                                     bias=ctx.eps128[:, 0:1], scale=1.0))
    A.wait(t5)
    t6 = A.mark(nc.scalar.activation(out=tm.rstd[:, 0:1], in_=tm.var[:, 0:1], func=AF.Exp,
                                     scale=-0.5))
    toks = []
    for out_ap in outs:
        V.wait(t6, *out_deps)
        toks.append(V.mark(nc.vector.tensor_scalar(out=out_ap, in0=src_ap,
                                                   scalar1=tm.nm[:, 0:1], scalar2=tm.rstd[:, 0:1],
                                                   op0=ALU.add, op1=ALU.mult)))
    return toks


def emit_tr32(ctx, src_fn, dst, dst_off, dep, n_chunks=32, only_t0=False):
    nc = ctx.nc
    last = None
    ngrp = (n_chunks + 7) // 8
    for grp in range(ngrp):
        bi = 0 if only_t0 else grp % 2
        bank = (ctx.T0, ctx.T1)[bi]
        ctx.T.wait(ctx.tbank_free[bi])
        if grp == 0:
            ctx.T.wait(*dep)
        ins = None
        cnt = min(8, n_chunks - grp * 8)
        for j in range(cnt):
            dc = grp * 8 + j
            ins = nc.tensor.transpose(bank[:, j * 128:(j + 1) * 128], src_fn(dc), ctx.ident[:])
        tpe = ctx.T.mark(ins)
        ctx.V.wait(tpe)
        offs = [dst_off(grp * 8 + j) for j in range(cnt)]
        if cnt == 1:
            out_ap = dst[:, offs[0]:offs[0] + 128]
            src_ap = bank[:, 0:128]
        else:
            step = offs[1] - offs[0]
            if step == 128:
                out_ap = dst[:, offs[0]:offs[0] + cnt * 128].rearrange("p (c t) -> p c t", c=cnt)
            else:
                rem = offs[0] % step
                out_ap = dst[:].rearrange("p (c t) -> p c t", t=step)[
                    :, offs[0] // step:offs[0] // step + cnt, rem:rem + 128]
            src_ap = bank[:, 0:cnt * 128].rearrange("p (c t) -> p c t", c=cnt)
        last = ctx.V.mark(nc.vector.tensor_copy(out=out_ap, in_=src_ap))
        ctx.tbank_free[bi] = last
    return last


def phase_barrier(ctx):
    qs = (ctx.T, ctx.V, ctx.A, ctx.G, ctx.LD, ctx.WD, ctx.SD)
    toks = [q.tok() for q in qs]
    for q in qs:
        q.wait(*toks)


def emit_const(ctx, name, arr, tile):
    nc = ctx.nc
    if arr.dtype == BF:
        t = nc.inline_tensor(np.ascontiguousarray(arr).view(np.uint16), name="c_" + name)
        return ctx.LD.mark(nc.sync.dma_start(out=tile[:].bitcast(mybir.dt.uint16), in_=t.ap()))
    t = nc.inline_tensor(np.ascontiguousarray(arr), name="c_" + name)
    return ctx.LD.mark(nc.sync.dma_start(out=tile[:], in_=t.ap()))



def emit_residual_pass(ctx, x_src, y_src, y_cast, x_dst, hT):
    nc = ctx.nc
    sc_ = nc.enter_named_scope("resid", False)
    barrier = ctx.SD.tok()
    with ExitStack() as st:
        c = _subctx(ctx, st)
        xt = [c.sb("rpx0", [128, D], F32), c.sb("rpx1", [128, D], F32)]
        hbf = c.sb("rph", [128, D], BF16)
        tm = [_ln_smalls(c, "rp0"), _ln_smalls(c, "rp1")]
        yt = [c.sb("rpy0", [128, D], F32), c.sb("rpy1", [128, D], F32)]
        tmy = [_ln_smalls(c, "rpy0"), _ln_smalls(c, "rpy1")]
        xfree = [None, None]
        yfree = [None, None]
        hfree = None
        for tc in range(NT):
            b = tc % 2
            rows = slice(tc * 128, (tc + 1) * 128)
            ctx.LD.wait(xfree[b], barrier)
            tx = ctx.LD.mark(nc.sync.dma_start(out=xt[b][:], in_=x_src[rows, :]))
            ldq = ctx.WD if y_cast else ctx.LD
            dma_eng = nc.gpsimd if y_cast else nc.sync
            ldq.wait(yfree[b], barrier)
            ty = ldq.mark(dma_eng.dma_start(out=yt[b][:], in_=y_src[rows, :]))
            t_ln = emit_ln(ctx, yt[b][:], hbf[:], tmy[b], [yt[b][:]], [ty],
                           out_deps=(), trash_dep=[hfree])[0]
            ctx.G.wait(t_ln, tx)
            tadd = ctx.G.mark(nc.gpsimd.tensor_tensor(out=yt[b][:], in0=yt[b][:], in1=xt[b][:], op=ALU.add))
            xfree[b] = tadd
            ctx.SD.wait(tadd)
            tst = ctx.SD.mark(nc.scalar.dma_start(out=x_dst[rows, :], in_=yt[b][:]))
            yfree[b] = tst
            if hT is not None:
                t_h = emit_ln(ctx, yt[b][:], hbf[:], tm[b], [hbf[:]], [tadd],
                              out_deps=[hfree], trash_dep=[hfree])[0]
                hfree = emit_tr32(ctx, lambda dc: hbf[:, dc * 128:(dc + 1) * 128],
                                  hT, lambda dc, tc=tc: dc * S + tc * 128, [t_h])
                yfree[b] = hfree
    phase_barrier(ctx)
    nc.leave_named_scope("resid", sc_[0], False)
    return ctx.SD.tok()



def emit_phaseA(ctx, x_src, wq, wk, wv, at_dram, adbg=None, wo=None, wo_bf=None):
    nc = ctx.nc
    T, V, A_, G, LD, WD, SD = ctx.T, ctx.V, ctx.A, ctx.G, ctx.LD, ctx.WD, ctx.SD
    B0, B1, B2, B3, B4, B5 = ctx.PB
    cos_np, sins_np = _rope_tables()
    with ExitStack() as st:
        c = _subctx(ctx, st)
        hT = c.sb("A_hT", [128, NCH * S], BF16)
        cosT = c.sb("A_cos", [128, S], BF16)
        sinT = c.sb("A_sin", [128, S], BF16)
        QT = [c.sb("A_QT0", [128, S], BF16), c.sb("A_QT1", [128, S], BF16)]
        KT = [c.sb("A_KT0", [128, S], BF16), c.sb("A_KT1", [128, S], BF16)]
        Vtm = [c.sb("A_Vtm0", [128, S], BF16), c.sb("A_Vtm1", [128, S], BF16)]
        scrA = c.sb("A_scr", [128, S], BF16)
        PT = [c.sb(f"A_PT{i}", [128, 512], BF16) for i in range(3)]
        fb = [c.sb(f"A_f{i}", [128, 512], F32) for i in range(2)]
        tl = [c.sb(f"A_tl{i}", [128, 512], F32) for i in range(2)]
        ATs = [c.sb("A_ATs0", [128, 512], BF16), c.sb("A_ATs1", [128, 512], BF16)]
        sm = c.sb("A_sm", [128, 64], F32)

        emit_const(ctx, "cosT", cos_np, cosT)
        tk = emit_const(ctx, "sinT", sins_np, sinT)
        V.wait(tk)
        t_ms = V.mark(nc.vector.memset(sm[0:1, 20:21], math.log(ISQ)))

        scA0 = nc.enter_named_scope("A0", False)
        with ExitStack() as st0:
            c0 = _subctx(ctx, st0)
            xt = c0.sb("A0_x", [128, D], F32)
            hbf = c0.sb("A0_h", [128, D], BF16)
            tm0 = _ln_smalls(c0, "A0")
            xfree = None
            hfree = None
            for tc in range(NT):
                LD.wait(xfree)
                tx = LD.mark(nc.sync.dma_start(out=xt[:], in_=x_src[tc * 128:(tc + 1) * 128, :]))
                t_h = emit_ln(ctx, xt[:], hbf[:], tm0, [hbf[:]], [tx],
                              out_deps=[hfree], trash_dep=[hfree])[0]
                hfree = emit_tr32(ctx, lambda dc: hbf[:, dc * 128:(dc + 1) * 128],
                                  hT, lambda dc, tc=tc: dc * S + tc * 128, [t_h])
                xfree = t_h

        phase_barrier(ctx)
        nc.leave_named_scope("A0", scA0[0], False)
        if adbg is not None and "hT" in adbg:
            ctx.SD.wait(hfree)
            ctx.SD.mark(nc.scalar.dma_start(out=adbg["hT"], in_=hT[:]))
            if adbg.get("stop") == "A0":
                return
        st_w = st.enter_context(ExitStack())
        cw_ = _subctx(ctx, st_w)
        slot = [cw_.sb(f"A_w{i}", [128, D], BF16) for i in range(3)]
        wraw = cw_.sb("A_wraw", [128, 2048], BF16)
        slot_ready = [None] * 3
        slot_free = [None] * 3
        state = Box()
        state.wraw_free = None

        def load_tensor(idx):
            h_, t_ = divmod(idx, 3)
            w = (wq, wk, wv)[t_]
            s_ = idx % 3
            cols = slice(h_ * 128, (h_ + 1) * 128)
            if t_ == 0 and wo_bf is not None:
                rows = slice(h_ * 128, (h_ + 1) * 128)
                for cc8 in range(8):
                    WD.mark(nc.gpsimd.dma_start(
                        out=wo_bf[rows, cc8 * 512:(cc8 + 1) * 512],
                        in_=wo[rows, cc8 * 512:(cc8 + 1) * 512]))
            if t_ == 2:
                WD.wait(slot_free[s_])
                tkk = None
                for part in range(2):
                    tkk = WD.mark(nc.gpsimd.dma_start(
                        out=slot[s_][:].rearrange("p (cc m) -> p cc m", cc=NCH)[:, part * 16:(part + 1) * 16, :],
                        in_=w[:, cols].rearrange("(cc p) m -> p cc m", p=128)[:, part * 16:(part + 1) * 16, :]))
                slot_ready[s_] = tkk
            else:
                tp = None
                for part in range(2):
                    WD.wait(state.wraw_free)
                    tkk = WD.mark(nc.gpsimd.dma_start(
                        out=wraw[:].rearrange("p (cc m) -> p cc m", cc=16),
                        in_=w[:, cols].rearrange("(cc p) m -> p cc m", p=128)[:, part * 16:(part + 1) * 16, :]))
                    G.wait(tkk)
                    G.wait(slot_free[s_])
                    tp = G.mark(nc.gpsimd.tensor_copy(
                        out=slot[s_][:, part * 2048:(part + 1) * 2048].rearrange(
                            "p (cc b hh i) -> p cc b hh i", cc=16, b=4, hh=2, i=16),
                        in_=wraw[:].rearrange("p (cc hh b i) -> p cc b hh i", cc=16, hh=2, b=4, i=16)))
                    state.wraw_free = tp
                slot_ready[s_] = tp

        for idx in range(3):
            load_tensor(idx)

        ps_free = {0: None, 1: None, 2: None}
        pt_free = [None, None, None]
        psS_free = [None, None]
        psO_free = None
        psSum_free = None
        tl_free = [None, None]
        rope_fb = [None, None]
        ATs_free = [None, None]
        scr_read_tok = [None]
        sum_bank = ctx.T1[:, 0:1024].bitcast(F32)
        V.wait(t_ms)
        t_nb = V.mark(nc.vector.memset(sm[:, 33:34], -50.0))

        def prep_gen(hh):
            par = hh % 2
            QTp, KTp, Vtmp = QT[par], KT[par], Vtm[par]
            stt_ = Box()
            stt_.qt_tok = stt_.kt_tok = stt_.vtm_tok = None
            vcopy_tok = None
            for g in range(4):
                gcol = slice(g * 512, (g + 1) * 512)
                proj = {}
                for t_, ps in ((0, B0), (1, B1), (2, B2)):
                    s_ = t_
                    T.wait(slot_ready[s_], ps_free[t_])
                    ins = None
                    for dc in range(NCH):
                        ins = nc.tensor.matmul(ps[:], slot[s_][:, dc * 128:(dc + 1) * 128],
                                               hT[:, dc * S + g * 512: dc * S + (g + 1) * 512],
                                               start=(dc == 0), stop=(dc == NCH - 1))
                    proj[t_] = T.mark(ins)
                    if g == 3:
                        slot_free[s_] = proj[t_]
                        if hh + 1 < H:
                            load_tensor(3 * (hh + 1) + t_)
                    yield
                for t_, ps, dstT in ((0, B0, QTp), (1, B1, KTp)):
                    V.wait(proj[t_], rope_fb[0])
                    tsq_ = V.mark(nc.vector.stream_shuffle(out=fb[0][:], in_=ps[:], mask=SHUF_MASK))
                    V.wait(rope_fb[1])
                    tcq_ = V.mark(nc.vector.tensor_tensor(out=fb[1][:], in0=ps[:], in1=cosT[:, gcol], op=ALU.mult))
                    ps_free[t_] = tcq_
                    G.wait(tsq_)
                    tu_ = G.mark(nc.gpsimd.tensor_tensor(out=fb[0][:], in0=fb[0][:], in1=sinT[:, gcol], op=ALU.mult))
                    V.wait(tu_, tcq_)
                    tok = V.mark(nc.vector.tensor_tensor(out=dstT[:, gcol], in0=fb[1][:], in1=fb[0][:], op=ALU.add))
                    rope_fb[0] = tok
                    rope_fb[1] = tok
                    if t_ == 0:
                        stt_.qt_tok = tok
                    else:
                        stt_.kt_tok = tok
                V.wait(proj[2], scr_read_tok[0])
                vcopy_tok = V.mark(nc.vector.tensor_copy(out=scrA[:, gcol], in_=B2[:]))
                ps_free[2] = vcopy_tok
                yield
            stt_.vtm_tok = emit_tr32(ctx, lambda kc: scrA[:, kc * 128:(kc + 1) * 128],
                                     Vtmp, lambda kc: kc * 128, [vcopy_tok], n_chunks=16,
                                     only_t0=True)
            scr_read_tok[0] = stt_.vtm_tok
            yield stt_

        def attn_gen(hh, stp):
            nonlocal psO_free, psSum_free
            par = hh % 2
            QTp, KTp, Vtmp = QT[par], KT[par], Vtm[par]
            for g in range(4):
                gcol = slice(g * 512, (g + 1) * 512)
                Kg = 4 * (g + 1)
                pv_last = None
                pt_toks = [None] * Kg
                for kt in range(Kg + 2):
                    if kt < Kg:
                        bs = kt % 2
                        b3 = kt % 3
                        psS = (B3, B5)[bs]
                        j = kt - 4 * g
                        off = 128 * max(j, 0)
                        T.wait(psS_free[bs], stp.qt_tok, stp.kt_tok)
                        tS = T.mark(nc.tensor.matmul(psS[:, off:512], KTp[:, kt * 128:(kt + 1) * 128],
                                                     QTp[:, g * 512 + off:(g + 1) * 512],
                                                     start=True, stop=True))
                        A_.wait(tS, t_nb, pt_free[b3])
                        tX = A_.mark(nc.scalar.activation(out=PT[b3][:, off:512], in_=psS[:, off:512],
                                                          func=AF.Exp, bias=sm[:, 33:34], scale=ISQ))
                        psS_free[bs] = tX
                        if j >= 0:
                            V.wait(tX)
                            tM = V.mark(nc.vector.tensor_tensor(out=PT[b3][:, off:off + 128],
                                                                in0=PT[b3][:, off:off + 128],
                                                                in1=ctx.tri01[:], op=ALU.mult))
                            pt_toks[kt] = tM
                        else:
                            pt_toks[kt] = tX
                        if adbg is not None and hh == 0 and g == 0 and kt == 0 and "esc0" in adbg:
                            ctx.SD.wait(pt_toks[kt])
                            ctx.SD.mark(nc.scalar.dma_start(out=adbg["pt0"], in_=PT[b3][:]))
                    if kt >= 2:
                        kp = kt - 2
                        bp = kp % 3
                        jp = kp - 4 * g
                        offp = 128 * max(jp, 0)
                        T.wait(pt_toks[kp])
                        if kp == 0:
                            T.wait(psO_free, psSum_free)
                        nc.tensor.matmul(B4[:, offp:512], Vtmp[:, kp * 128:(kp + 1) * 128],
                                         PT[bp][:, offp:512],
                                         start=(kp == 0), stop=(kp == Kg - 1))
                        ins = nc.tensor.matmul(sum_bank[:, offp:512], ctx.ones128_bf[:],
                                               PT[bp][:, offp:512],
                                               start=(kp == 0), stop=(kp == Kg - 1))
                        pv_last = T.mark(ins)
                        pt_free[bp] = pv_last
                    yield
                tp_ = g % 2
                V.wait(pv_last, tl_free[tp_])
                tR = V.mark(nc.vector.reciprocal(out=tl[tp_][:], in_=sum_bank))
                psSum_free = tR
                yield
                V.wait(ATs_free[tp_])
                tAT = V.mark(nc.vector.tensor_tensor(out=ATs[tp_][:], in0=B4[:], in1=tl[tp_][:], op=ALU.mult))
                psO_free = tAT
                tl_free[tp_] = tAT
                if adbg is not None and hh == 0 and g == 0 and "rcb0" in adbg:
                    ctx.SD.wait(tR)
                    ctx.SD.mark(nc.scalar.dma_start(out=adbg["rcb0"], in_=tl[0][:]))
                SD.wait(tAT)
                ATs_free[tp_] = SD.mark(nc.scalar.dma_start(
                    out=at_dram[hh * 128:(hh + 1) * 128, g * 512:(g + 1) * 512], in_=ATs[tp_][:]))
                yield

        def run_gen(gen):
            r = None
            for r in gen:
                pass
            return r

        stp = run_gen(prep_gen(0))
        if adbg is not None and "QT" in adbg:
            ctx.SD.wait(stp.qt_tok, stp.kt_tok, stp.vtm_tok)
            ctx.SD.mark(nc.scalar.dma_start(out=adbg["QT"], in_=QT[0][:]))
            ctx.SD.mark(nc.scalar.dma_start(out=adbg["KT"], in_=KT[0][:]))
            ctx.SD.mark(nc.scalar.dma_start(out=adbg["Vtm"], in_=Vtm[0][:]))
            ctx.SD.mark(nc.scalar.dma_start(out=adbg["sm"], in_=sm[:]))
            if adbg.get("stop") == "qk":
                return
        for h in range(H):
            ag = attn_gen(h, stp)
            pg = prep_gen(h + 1) if h + 1 < H else None
            stn = None
            alive_a = True
            while alive_a or pg is not None:
                if pg is not None:
                    try:
                        r = next(pg)
                        if r is not None:
                            stn = r
                            pg = None
                    except StopIteration:
                        pg = None
                for _ in range(4):
                    if alive_a:
                        try:
                            next(ag)
                        except StopIteration:
                            alive_a = False
            stp = stn
    phase_barrier(ctx)
    return ctx.SD.tok()


def esc_of(fb, bs):
    return fb[2 + bs]



def emit_wo(ctx, at_dram, wo_bf, y_dram, at_ready, wq_int, wscales, wqd):
    nc = ctx.nc
    T, V, A_, G, LD, WD, SD = ctx.T, ctx.V, ctx.A, ctx.G, ctx.LD, ctx.WD, ctx.SD
    rd_guard = (ctx.LD.tok(), ctx.WD.tok())
    B0, B1 = ctx.PB[0], ctx.PB[1]
    SW = 256
    NSL = D // SW
    with ExitStack() as st:
        c = _subctx(ctx, st)
        ATb = c.sb("WO_at", [128, NCH * S], BF16)
        wob = [c.sb("WO_w0", [128, NCH * 512], BF16)]
        ysb = [c.sb("WO_y0", [128, 512], F32), c.sb("WO_y1", [128, 512], F32)]
        qt = [c.sb(f"WO_qt{i}", [128, 512], F32) for i in range(2)]
        sct = [c.sb(f"WO_sct{i}", [128, 256], F32) for i in range(2)]
        wbs = c.sb("WO_wbs", [128, 2048], BF16)
        dq2 = [[c.sb(f"WO_dq{k}_{i}", [128, SW], F32) for i in range(4)] for k in range(4)]
        one128f = c.sb("WO_one", [128, 1], F32)
        ctx.V.mark(nc.vector.memset(one128f[:], 1.0))

        LD.wait(at_ready)
        t_at = None
        for part in range(8):
            t_at = LD.mark(nc.sync.dma_start(
                out=ATb[:].rearrange("p (cc t) -> p cc t", cc=NCH)[:, part * 4:(part + 1) * 4, :],
                in_=at_dram.rearrange("(cc p) t -> p cc t", p=128)[:, part * 4:(part + 1) * 4, :]))
        w_ready = [None]
        w_free = [None]
        ev_free = [None, None]
        st_free = [None, None]

        def load_wo(oc):
            b = 0
            LD.wait(w_free[b])
            tk = None
            for part in range(4):
                tk = LD.mark(nc.sync.dma_start(
                    out=wob[b][:].rearrange("p (cc n) -> p cc n", cc=NCH)[:, part * 8:(part + 1) * 8, :],
                    in_=wo_bf[:, oc * 512:(oc + 1) * 512].rearrange("(cc p) n -> p cc n", p=128)[:, part * 8:(part + 1) * 8, :]))
            w_ready[b] = tk

        def wo_gen():
            load_wo(0)
            for oc in range(8):
                b = 0
                last_mm = None
                for tc in range(NT):
                    pb = tc % 2
                    ps = (B0, B1)[pb]
                    T.wait(w_ready[b], t_at, ev_free[pb])
                    ins = None
                    for dc in range(NCH):
                        ins = nc.tensor.matmul(ps[:], ATb[:, dc * S + tc * 128: dc * S + (tc + 1) * 128],
                                               wob[b][:, dc * 512:(dc + 1) * 512],
                                               start=(dc == 0), stop=(dc == NCH - 1))
                    tmm = T.mark(ins)
                    last_mm = tmm
                    V.wait(tmm, st_free[pb])
                    tev = V.mark(nc.vector.tensor_copy(out=ysb[pb][:], in_=ps[:]))
                    ev_free[pb] = tev
                    SD.wait(tev, *rd_guard)
                    st_free[pb] = SD.mark(nc.scalar.dma_start(
                        out=y_dram[tc * 128:(tc + 1) * 128, oc * 512:(oc + 1) * 512], in_=ysb[pb][:]))
                    yield
                w_free[b] = last_mm
                if oc + 1 < 8:
                    load_wo(oc + 1)

        def dq_gen():
            NS = 512
            tj = [None] * NS
            t4a = [None] * NS
            t5a = [None] * NS
            tga = [None] * NS
            tta = [None] * NS
            tua = [None] * NS
            twa = [None] * NS
            t10a = [None] * NS
            qtok = {}
            tscs = [None] * 32
            wbs_store = None

            def qsrc(B):
                rc, b = divmod(B, 8)
                return wq_int[rc * 128:(rc + 1) * 128, b * 512:(b + 1) * 512]

            for B in (0, 1):
                qtok[B] = WD.mark(nc.gpsimd.dma_start(out=qt[B % 2][:], in_=qsrc(B)))
            techo = WD.mark(nc.gpsimd.dma_start(
                out=qt[0][:, 508:512], in_=wq_int[0:128, 508:512]))
            tscs[0] = LD.mark(nc.sync.dma_start(out=sct[0][:], in_=wscales[0:128, :]))
            for i in range(NS + 4):
                p = i - 4
                if p >= 0:
                    rcp, slp = divmod(p, 16)
                    colq = slice((slp % 8) * SW, (slp % 8 + 1) * SW)
                    jb = dq2[p % 4][1]
                    V.wait(twa[p], tscs[rcp], wbs_store)
                    t10a[p] = V.mark(nc.vector.tensor_tensor(
                        out=wbs[:, colq].rearrange("p (a b) -> p a b", b=16),
                        in0=jb[:].rearrange("p (a b) -> p a b", b=16),
                        in1=sct[rcp % 2][:, slp * 16:(slp + 1) * 16].unsqueeze(2).broadcast_to([128, 16, 16]),
                        op=ALU.mult))
                    if slp == 7 or slp == 15:
                        SD.wait(t10a[p])
                        half = slice(0, 2048) if slp == 7 else slice(2048, 4096)
                        wbs_store = SD.mark(nc.scalar.dma_start(
                            out=wqd[rcp * 128:(rcp + 1) * 128, half], in_=wbs[:]))
                if i < NS:
                    B = i // 2
                    qb = qt[B % 2]
                    qoff = (i % 2) * 256
                    sb_, jb, r1, r2 = dq2[i % 4]
                    V.wait(qtok[B], techo if i < 2 else None,
                           twa[i - 4] if i >= 4 else None)
                    t2s = V.mark(nc.vector.tensor_single_scalar(
                        out=sb_[:], in_=qb[:, qoff:qoff + 256], scalar=8.0, op=ALU.is_ge))
                    V.wait(t2s)
                    tj[i] = V.mark(nc.vector.scalar_tensor_tensor(
                        out=jb[:], in0=sb_[:], scalar=-8.0,
                        in1=qb[:, qoff:qoff + 256], op0=ALU.mult, op1=ALU.add))
                    if i % 2 == 1 and B + 2 < 256:
                        WD.wait(tj[i])
                        qtok[B + 2] = WD.mark(nc.gpsimd.dma_start(out=qt[B % 2][:], in_=qsrc(B + 2)))
                    rci = i // 16
                    if i % 16 == 8 and rci + 1 < 32:
                        LD.wait(t10a[(rci - 1) * 16 + 15] if rci >= 1 else None)
                        tscs[rci + 1] = LD.mark(nc.sync.dma_start(
                            out=sct[(rci + 1) % 2][:], in_=wscales[(rci + 1) * 128:(rci + 2) * 128, :]))
                q2 = i - 2
                if 0 <= q2 < NS:
                    sb_, jb, r1, r2 = dq2[q2 % 4]
                    V.wait(t4a[q2])
                    tta[q2] = V.mark(nc.vector.scalar_tensor_tensor(
                        out=r1[:], in0=jb[:], scalar=0.5, in1=r1[:], op0=ALU.mult, op1=ALU.add))
                a = i - 1
                if 0 <= a < NS:
                    sb_, jb, r1, r2 = dq2[a % 4]
                    A_.wait(tj[a], tua[a - 4] if a >= 4 else None)
                    t4a[a] = A_.mark(nc.scalar.activation(out=r1[:], in_=jb[:], func=AF.Relu,
                                                          bias=ctx.cm2[:, 0:1], scale=0.5))
                    tga[a] = A_.mark(nc.scalar.activation(out=sb_[:], in_=sb_[:], func=AF.Sign,
                                                          bias=one128f[:, 0:1], scale=-2.0))
                    t5a[a] = A_.mark(nc.scalar.activation(out=r2[:], in_=jb[:], func=AF.Relu,
                                                          bias=ctx.cm6[:, 0:1], scale=1.0))
                g3 = i - 3
                if 0 <= g3 < NS:
                    sb_, jb, r1, r2 = dq2[g3 % 4]
                    G.wait(tta[g3], t5a[g3], t10a[g3 - 4] if g3 >= 4 else None)
                    tua[g3] = G.mark(nc.gpsimd.tensor_tensor(out=jb[:], in0=r1[:], in1=r2[:], op=ALU.add))
                    V.wait(tua[g3], tga[g3])
                    twa[g3] = V.mark(nc.vector.tensor_tensor(out=jb[:], in0=jb[:], in1=sb_[:], op=ALU.mult))
                yield

        wg, dg = wo_gen(), dq_gen()
        alive_w = alive_d = True
        while alive_w or alive_d:
            if alive_w:
                try:
                    next(wg)
                except StopIteration:
                    alive_w = False
            for _ in range(4):
                if alive_d:
                    try:
                        next(dg)
                    except StopIteration:
                        alive_d = False
    phase_barrier(ctx)
    return ctx.SD.tok()


def emit_phaseB(ctx, x_src, y_prev, wqd, y_dram):
    nc = ctx.nc
    T, V, A_, G, LD, WD, SD = ctx.T, ctx.V, ctx.A, ctx.G, ctx.LD, ctx.WD, ctx.SD
    B0, B1 = ctx.PB[0], ctx.PB[1]
    with ExitStack() as st:
        c = _subctx(ctx, st)
        hT = c.sb("B_hT", [128, NCH * S], BF16)
        emit_residual_pass(ctx, x_src[0], y_prev, False, x_src[1], hT)
        rd_guard = (ctx.LD.tok(), ctx.WD.tok())

        WT = c.sb("B_WT", [128, NCH * 512], BF16)
        wbf = [c.sb("B_w0", [128, D], BF16), c.sb("B_w1", [128, D], BF16)]
        ysb = [c.sb("B_y0", [128, 512], BF16), c.sb("B_y1", [128, 512], BF16)]
        wt_free = None
        ev_free = [None, None]
        st_free = [None, None]
        w_free = [None, None]
        for oc in range(8):
            tr_tok = None
            for sc_ in range(4):
                wb = (oc * 4 + sc_) % 2
                row0 = oc * 512 + sc_ * 128
                LD.wait(w_free[wb])
                tw = LD.mark(nc.sync.dma_start(out=wbf[wb][:], in_=wqd[row0:row0 + 128, :]))
                tr_tok = emit_tr32(ctx, lambda ic, wb=wb: wbf[wb][:, ic * 128:(ic + 1) * 128],
                                   WT, lambda ic, sc_=sc_: ic * 512 + sc_ * 128,
                                   [tw, wt_free], n_chunks=32)
                w_free[wb] = tr_tok
            last_mm = None
            for tc in range(NT):
                pb = tc % 2
                ps = (B0, B1)[pb]
                T.wait(tr_tok, ev_free[pb])
                ins = None
                for dc in range(NCH):
                    ins = nc.tensor.matmul(ps[:], hT[:, dc * S + tc * 128: dc * S + (tc + 1) * 128],
                                           WT[:, dc * 512:(dc + 1) * 512],
                                           start=(dc == 0), stop=(dc == NCH - 1))
                tmm = T.mark(ins)
                last_mm = tmm
                A_.wait(tmm, st_free[pb])
                tev = A_.mark(nc.scalar.activation(out=ysb[pb][:], in_=ps[:], func=AF.Copy, scale=1.0))
                ev_free[pb] = tev
                SD.wait(tev, *rd_guard)
                st_free[pb] = SD.mark(nc.scalar.dma_start(
                    out=y_dram[tc * 128:(tc + 1) * 128, oc * 512:(oc + 1) * 512], in_=ysb[pb][:]))
            wt_free = last_mm
    phase_barrier(ctx)
    return ctx.SD.tok()



def emit_phaseC(ctx, x_src, y_prev, cmask, cw, cbias, y_dram):
    nc = ctx.nc
    T, V, A_, G, LD, WD, SD = ctx.T, ctx.V, ctx.A, ctx.G, ctx.LD, ctx.WD, ctx.SD
    B0, B1, B4, B5 = ctx.PB[0], ctx.PB[1], ctx.PB[4], ctx.PB[5]
    with ExitStack() as st:
        c = _subctx(ctx, st)
        hT = c.sb("C_hT", [128, NCH * S], BF16)
        emit_residual_pass(ctx, x_src[0], y_prev, True, x_src[1], hT)
        rd_guard = (ctx.LD.tok(), ctx.WD.tok())

        WT = c.sb("C_WT", [128, NCH * 512], BF16)
        mt = c.sb("C_mt", [128, D], I32)
        wbf = [c.sb("C_w0", [128, D], BF16), c.sb("C_w1", [128, D], BF16)]
        wsum = c.sb("C_ws", [128, 128], BF16)
        bm01 = c.sb("C_bm", [4, 128], BF16)
        cbb = c.sb("C_cb", [1, D], BF16)
        ysb = [c.sb("C_y0", [128, 512], F32), c.sb("C_y1", [128, 512], F32)]

        t_cb = WD.mark(nc.gpsimd.dma_start(out=cbb[0:1, :], in_=cbias.rearrange("(o n) -> o n", o=1)))
        mt_free = None
        ws_free = None
        bm_free = None
        wt_free = None
        psX_free = None
        psM_free = None
        ev_free = [None, None]
        st_free = [None, None]
        w_free = [None, None]
        for oc in range(8):
            tr_tok = None
            for sc_ in range(4):
                wb = (oc * 4 + sc_) % 2
                row0 = oc * 512 + sc_ * 128
                LD.wait(mt_free)
                tmk = LD.mark(nc.sync.dma_start(out=mt[:], in_=cmask[row0:row0 + 128, :]))
                WD.wait(w_free[wb])
                tw = WD.mark(nc.gpsimd.dma_start(out=wbf[wb][:], in_=cw[row0:row0 + 128, :]))
                V.wait(tmk, ws_free)
                with nc.allow_low_precision(reason="mask block-sum only tested >0"):
                    tws = V.mark(nc.vector.tensor_reduce(out=wsum[:, :],
                                                         in_=mt[:].rearrange("p (a b) -> p a b", b=32),
                                                         axis=AX.X, op=ALU.add))
                mt_free = tws
                T.wait(tws, psM_free)
                tbs = T.mark(nc.tensor.matmul(B4[0:4, 0:128], ctx.blk4_bf[:], wsum[:, :], start=True, stop=True))
                ws_free = tbs
                V.wait(tbs, bm_free)
                tbm = V.mark(nc.vector.tensor_single_scalar(out=bm01[0:4, :], in_=B4[0:4, 0:128],
                                                            scalar=0.0, op=ALU.is_gt))
                psM_free = tbm
                T.wait(tbm, psX_free)
                tex = T.mark(nc.tensor.matmul(B5[:, 0:128], ctx.e4_bf[:], bm01[0:4, :], start=True, stop=True))
                bm_free = tex
                V.wait(tex, tw)
                tmask = V.mark(nc.vector.tensor_tensor(
                    out=wbf[wb][:].rearrange("p (a b) -> p a b", b=32),
                    in0=wbf[wb][:].rearrange("p (a b) -> p a b", b=32),
                    in1=B5[:, 0:128].unsqueeze(2).broadcast_to([128, 128, 32]),
                    op=ALU.mult))
                psX_free = tmask
                tr_tok = emit_tr32(ctx, lambda ic, wb=wb: wbf[wb][:, ic * 128:(ic + 1) * 128],
                                   WT, lambda ic, sc_=sc_: ic * 512 + sc_ * 128,
                                   [tmask, wt_free], n_chunks=32)
                w_free[wb] = tr_tok
            last_mm = None
            for tc in range(NT):
                pb = tc % 2
                ps = (B0, B1)[pb]
                T.wait(tr_tok, ev_free[pb], t_cb)
                for dc in range(NCH):
                    nc.tensor.matmul(ps[:], hT[:, dc * S + tc * 128: dc * S + (tc + 1) * 128],
                                     WT[:, dc * 512:(dc + 1) * 512],
                                     start=(dc == 0), stop=False)
                ins = nc.tensor.matmul(ps[:], ctx.ones1_bf[:], cbb[0:1, oc * 512:(oc + 1) * 512],
                                       start=False, stop=True)
                tmm = T.mark(ins)
                last_mm = tmm
                A_.wait(tmm, st_free[pb])
                tev = A_.mark(nc.scalar.activation(out=ysb[pb][:], in_=ps[:], func=AF.Copy, scale=1.0))
                ev_free[pb] = tev
                SD.wait(tev, *rd_guard)
                st_free[pb] = SD.mark(nc.scalar.dma_start(
                    out=y_dram[tc * 128:(tc + 1) * 128, oc * 512:(oc + 1) * 512], in_=ysb[pb][:]))
            wt_free = last_mm
    phase_barrier(ctx)
    return ctx.SD.tok()



def emit_phaseD(ctx, x_src, y_prev, dw, dbias, dgw, dics, docs, y_dram, out_dram):
    nc = ctx.nc
    T, V, A_, G, LD, WD, SD = ctx.T, ctx.V, ctx.A, ctx.G, ctx.LD, ctx.WD, ctx.SD
    B0, B1, B2, B3, B4, B5 = ctx.PB
    with ExitStack() as st:
        c = _subctx(ctx, st)
        routT = c.sb("D_routT", [8, S], F32)
        emit_residual_pass(ctx, x_src[0], y_prev, False, x_src[1], None)
        rd_guard = (ctx.LD.tok(), ctx.WD.tok())
        with ExitStack() as stm:
            cm = _subctx(ctx, stm)
            XST = cm.sb("D_XST", [128, NCH * S], BF16)
            with ExitStack() as st1:
                c1 = _subctx(ctx, st1)
                xt = [c1.sb("D_x0", [128, D], F32)]
                hbf = c1.sb("D_h", [128, D], BF16)
                hTD = c1.sb("D_hTD", [128, D], BF16)
                xs = c1.sb("D_xs", [128, D], BF16)
                gwT = c1.sb("D_gwT", [128, 256], BF16)
                ctx.identf = c1.sb("D_identf", [128, 128], F32)
                emit_const(ctx, "identf", np.eye(128, dtype=np.float32), ctx.identf)
                icsb = c1.sb("D_ics", [8, D], F32)
                rg = c1.sb("D_rg", [128, 8], F32)
                rbf = c1.sb("D_rbf", [128, 8], F32)
                smD = c1.sb("D_sm", [128, 2], F32)
                tmD = [_ln_smalls(c1, "D0"), _ln_smalls(c1, "D1")]
                gw8 = c1.sb("D_gw8", [8, D], BF16)

                t_ic = ctx.LD.mark(nc.sync.dma_start(out=icsb[0:8, :], in_=dics))
                t_gw = WD.mark(nc.gpsimd.dma_start(out=gw8[0:8, :], in_=dgw))
                ctx.T.wait(ctx.tbank_free[1], t_gw)
                ins_g = None
                for dc in range(NCH):
                    ins_g = nc.tensor.transpose(ctx.T1[:, dc * 8:(dc + 1) * 8],
                                                gw8[0:8, dc * 128:(dc + 1) * 128], ctx.ident[0:8, 0:8])
                t_gt = T.mark(ins_g)
                V.wait(t_gt)
                gwt_tok = V.mark(nc.vector.tensor_copy(out=gwT[:, 0:256], in_=ctx.T1[:, 0:256]))
                ctx.tbank_free[1] = gwt_tok
                xfree = [None, None]
                hfree = None
                hTD_free = None
                xs_free = None
                psG_free = None
                psI_free = [None, None]
                rt_free = None
                state5 = Box()
                state5.b5_free = None
                for tc in range(NT):
                    b = 0
                    rows = slice(tc * 128, (tc + 1) * 128)
                    LD.wait(xfree[b])
                    tx = LD.mark(nc.sync.dma_start(out=xt[b][:], in_=x_src[1][rows, :]))
                    toks = emit_ln(ctx, xt[b][:], hbf[:], tmD[b], [hbf[:], xt[b][:]], [tx],
                                   out_deps=[hfree], trash_dep=[hfree])
                    t_hb, t_hf = toks
                    hfree = t_hb
                    thT = emit_tr32(ctx, lambda dc: hbf[:, dc * 128:(dc + 1) * 128],
                                    hTD, lambda dc: dc * 128, [t_hb, hTD_free])
                    hfree = thT
                    T.wait(thT, gwt_tok, psG_free)
                    ins = None
                    for dc in range(NCH):
                        ins = nc.tensor.matmul(B4[:, 0:8], hTD[:, dc * 128:(dc + 1) * 128],
                                               gwT[:, dc * 8:(dc + 1) * 8],
                                               start=(dc == 0), stop=(dc == NCH - 1))
                    t_g = T.mark(ins)
                    V.wait(t_g)
                    t_mx = V.mark(nc.vector.tensor_reduce(out=smD[:, 0:1], in_=B4[:, 0:8],
                                                          axis=AX.X, op=ALU.max, negate=True))
                    A_.wait(t_mx)
                    t_eg = A_.mark(nc.scalar.activation(out=rg[:, :], in_=B4[:, 0:8], func=AF.Exp,
                                                        bias=smD[:, 0:1], scale=1.0))
                    psG_free = t_eg
                    V.wait(t_eg)
                    t_sm = V.mark(nc.vector.tensor_reduce(out=smD[:, 1:2], in_=rg[:, :], axis=AX.X, op=ALU.add))
                    V.wait(t_sm)
                    V.e.wait_ge(t_sm[0], t_sm[1])
                    t_rc = V.mark(nc.vector.reciprocal(out=smD[:, 0:1], in_=smD[:, 1:2]))
                    V.wait(t_rc)
                    V.e.wait_ge(t_rc[0], t_rc[1])
                    t_rn = V.mark(nc.vector.tensor_scalar_mul(out=rbf[:, :], in0=rg[:, :], scalar1=smD[:, 0:1]))
                    ctx.T.wait(state5.b5_free, t_rn)
                    t_tr = T.mark(nc.tensor.transpose(ctx.PB[5][0:8, 0:128], rbf[:, :], ctx.identf[:]))
                    V.wait(t_tr, rt_free)
                    t_rt = V.mark(nc.vector.tensor_copy(out=routT[0:8, tc * 128:(tc + 1) * 128],
                                                        in_=ctx.PB[5][0:8, 0:128]))
                    state5.b5_free = t_rt
                    xs_toks = []
                    for n8 in range(8):
                        pb = n8 % 2
                        psI = (B2, B3)[pb]
                        T.wait(t_rt, t_ic, psI_free[pb])
                        t_i = T.mark(nc.tensor.matmul(psI[:], routT[0:8, tc * 128:(tc + 1) * 128],
                                                      icsb[0:8, n8 * 512:(n8 + 1) * 512],
                                                      start=True, stop=True))
                        V.wait(t_i, t_hf, xs_free)
                        t_x = V.mark(nc.vector.tensor_tensor(out=xs[:, n8 * 512:(n8 + 1) * 512],
                                                             in0=xt[b][:, n8 * 512:(n8 + 1) * 512],
                                                             in1=psI[:], op=ALU.mult))
                        psI_free[pb] = t_x
                        xs_toks.append(t_x)
                    xfree[b] = xs_toks[-1]
                    xs_free = emit_tr32(ctx, lambda dc: xs[:, dc * 128:(dc + 1) * 128],
                                        XST, lambda dc, tc=tc: dc * S + tc * 128, [xs_toks[-1]])
                    hTD_free = t_g
                phase_barrier(ctx)
            with ExitStack() as st2:
                c2 = _subctx(ctx, st2)
                WT = c2.sb("D_WT", [128, NCH * 512], BF16)
                wbf = [c2.sb(f"D_w{i}", [128, D], BF16) for i in range(3)]
                ysb = [c2.sb("D_y0", [128, 512], F32), c2.sb("D_y1", [128, 512], F32)]
                wt_free = None
                ev_free = [None, None]
                st_free = [None, None]
                wb_free = [None, None, None]
                sg_rdy = {}
                wb_of = {}
                slot_ctr = [0]

                def prep_chunk(oc_, sc_):
                    wb = slot_ctr[0] % 3
                    slot_ctr[0] += 1
                    row0 = oc_ * 512 + sc_ * 128
                    WD.wait(wb_free[wb])
                    tw = WD.mark(nc.gpsimd.dma_start(out=wbf[wb][:], in_=dw[row0:row0 + 128, :]))
                    A_.wait(tw, wb_free[wb])
                    sg_rdy[(oc_, sc_)] = A_.mark(nc.scalar.activation(
                        out=wbf[wb][:], in_=wbf[wb][:], func=AF.Sign))
                    wb_of[(oc_, sc_)] = wb

                for sc_ in range(3):
                    prep_chunk(0, sc_)
                for oc in range(8):
                    tr_tok = None
                    for sc_ in range(4):
                        wb = wb_of[(oc, sc_)]
                        tr_tok = emit_tr32(ctx, lambda ic, wb=wb: wbf[wb][:, ic * 128:(ic + 1) * 128],
                                           WT, lambda ic, sc_=sc_: ic * 512 + sc_ * 128,
                                           [sg_rdy[(oc, sc_)], wt_free], n_chunks=32)
                        wb_free[wb] = tr_tok
                        if sc_ == 0:
                            prep_chunk(oc, 3)
                    last_mm = None
                    for tc in range(NT):
                        if oc + 1 < 8 and tc in (8, 11, 14):
                            prep_chunk(oc + 1, (tc - 8) // 3)
                        pb = tc % 2
                        ps = (B0, B1)[pb]
                        T.wait(tr_tok, ev_free[pb])
                        ins = None
                        for dc in range(NCH):
                            ins = nc.tensor.matmul(ps[:], XST[:, dc * S + tc * 128: dc * S + (tc + 1) * 128],
                                                   WT[:, dc * 512:(dc + 1) * 512],
                                                   start=(dc == 0), stop=(dc == NCH - 1))
                        tmm = T.mark(ins)
                        last_mm = tmm
                        A_.wait(tmm, st_free[pb])
                        tev = A_.mark(nc.scalar.activation(out=ysb[pb][:], in_=ps[:], func=AF.Copy, scale=1.0))
                        ev_free[pb] = tev
                        SD.wait(tev, *rd_guard)
                        st_free[pb] = SD.mark(nc.scalar.dma_start(
                            out=y_dram[tc * 128:(tc + 1) * 128, oc * 512:(oc + 1) * 512], in_=ysb[pb][:]))
                    wt_free = last_mm
                phase_barrier(ctx)
            y_ready = ctx.SD.tok()
        with ExitStack() as st3:
            c3 = _subctx(ctx, st3)
            yt = [c3.sb("De_y0", [128, D], F32), c3.sb("De_y1", [128, D], F32)]
            xt = [c3.sb("De_x0", [128, D], F32), c3.sb("De_x1", [128, D], F32)]
            trash = c3.sb("De_tr", [128, D], BF16)
            biasB = c3.sb("De_bias", [128, D], F32)
            ocsb = c3.sb("De_ocs", [8, D], F32)
            dbb = c3.sb("De_db", [1, D], BF16)
            tmE = [_ln_smalls(c3, "De0"), _ln_smalls(c3, "De1")]
            t_oc = ctx.LD.mark(nc.sync.dma_start(out=ocsb[0:8, :], in_=docs))
            t_db = WD.mark(nc.gpsimd.dma_start(out=dbb[0:1, :], in_=dbias.rearrange("(o n) -> o n", o=1)))
            bias_tok = None
            for n8 in range(8):
                T.wait(t_db, bias_tok)
                t_bb = T.mark(nc.tensor.matmul(B5[:], ctx.ones1_bf[:], dbb[0:1, n8 * 512:(n8 + 1) * 512],
                                               start=True, stop=True))
                A_.wait(t_bb)
                bias_tok = A_.mark(nc.scalar.activation(out=biasB[:, n8 * 512:(n8 + 1) * 512],
                                                        in_=B5[:], func=AF.Copy, scale=1.0))
            yfree = [None, None]
            xfree = [None, None]
            psO_free = [None, None]
            for tc in range(NT):
                b = tc % 2
                rows = slice(tc * 128, (tc + 1) * 128)
                LD.wait(yfree[b], y_ready)
                ty = LD.mark(nc.sync.dma_start(out=yt[b][:], in_=y_dram[rows, :]))
                LD.wait(xfree[b])
                tx = LD.mark(nc.sync.dma_start(out=xt[b][:], in_=x_src[1][rows, :]))
                mul_toks = []
                for n8 in range(8):
                    pb = n8 % 2
                    psO = (B2, B3)[pb]
                    cols = slice(n8 * 512, (n8 + 1) * 512)
                    T.wait(psO_free[pb], t_oc)
                    t_o = T.mark(nc.tensor.matmul(psO[:], routT[0:8, tc * 128:(tc + 1) * 128],
                                                  ocsb[0:8, n8 * 512:(n8 + 1) * 512],
                                                  start=True, stop=True))
                    V.wait(t_o, ty, bias_tok)
                    t_m = V.mark(nc.vector.tensor_tensor(out=yt[b][:, cols], in0=yt[b][:, cols],
                                                         in1=psO[:], op=ALU.mult))
                    psO_free[pb] = t_m
                    V.wait(t_m)
                    t_m = V.mark(nc.vector.tensor_tensor(out=yt[b][:, cols], in0=yt[b][:, cols],
                                                         in1=biasB[:, cols], op=ALU.add))
                    mul_toks.append(t_m)
                t_ba = mul_toks[-1]
                t_ln = emit_ln(ctx, yt[b][:], trash[:], tmE[b], [yt[b][:]], [t_ba],
                               out_deps=(), trash_dep=())[0]
                G.wait(t_ln, tx)
                t_add = G.mark(nc.gpsimd.tensor_tensor(out=yt[b][:], in0=yt[b][:], in1=xt[b][:], op=ALU.add))
                xfree[b] = t_add
                SD.wait(t_add)
                yfree[b] = SD.mark(nc.scalar.dma_start(out=out_dram[rows, :], in_=yt[b][:]))
        phase_barrier(ctx)
    return ctx.SD.tok()



def build_program(upto="D", dbg=False):
    _NAME_CTR[0] = 0
    nc = bass.Bass("TRN2", target_bir_lowering=False)
    x_in = nc.dram_tensor("x", [S, D], F32, kind="ExternalInput").ap()
    wq = nc.dram_tensor("wq", [D, D], F32, kind="ExternalInput").ap()
    wk = nc.dram_tensor("wk", [D, D], F32, kind="ExternalInput").ap()
    wv = nc.dram_tensor("wv", [D, D], F32, kind="ExternalInput").ap()
    wo = nc.dram_tensor("wo", [D, D], F32, kind="ExternalInput").ap()
    fq = nc.dram_tensor("fq", [D, D], I32, kind="ExternalInput").ap()
    fsc = nc.dram_tensor("fsc", [D, 256], F32, kind="ExternalInput").ap()
    cm = nc.dram_tensor("cm", [D, D], I32, kind="ExternalInput").ap()
    cw = nc.dram_tensor("cw", [D, D], F32, kind="ExternalInput").ap()
    cb = nc.dram_tensor("cb", [D], F32, kind="ExternalInput").ap()
    dw = nc.dram_tensor("dw", [D, D], F32, kind="ExternalInput").ap()
    db = nc.dram_tensor("db", [D], F32, kind="ExternalInput").ap()
    dgw = nc.dram_tensor("dgw", [8, D], F32, kind="ExternalInput").ap()
    dics = nc.dram_tensor("dics", [8, D], F32, kind="ExternalInput").ap()
    docs = nc.dram_tensor("docs", [8, D], F32, kind="ExternalInput").ap()
    out = nc.dram_tensor("out", [S, D], F32, kind="ExternalOutput").ap()

    kd = "ExternalOutput" if dbg else "Internal"
    xB = nc.dram_tensor("xB", [S, D], F32, kind=kd).ap()
    xC = nc.dram_tensor("xC", [S, D], F32, kind=kd).ap()
    yF = nc.dram_tensor("yF", [S, D], F32, kind=kd).ap()
    yB = nc.dram_tensor("yBd", [S, D], BF16, kind=kd).ap()
    wqd = nc.dram_tensor("wqd", [D, D], BF16, kind=kd).ap()
    wo_bf = nc.dram_tensor("wo_bf", [D, D], BF16, kind="Internal").ap()
    at = nc.dram_tensor("atd", [D, S], BF16, kind=kd).ap()

    with ExitStack() as stack:
        ctx = Box()
        ctx.nc = nc
        ctx.sb = lambda name, shape, dt: stack.enter_context(nc.sbuf_tensor(name, shape, dt))
        ctx.T = Qw(stack, nc, nc.tensor, "sPE")
        ctx.V = Qw(stack, nc, nc.vector, "sV")
        ctx.A = Qw(stack, nc, nc.scalar, "sA")
        ctx.G = Qw(stack, nc, nc.gpsimd, "sG")
        ctx.LD = Qw(stack, nc, nc.sync, "sLD", dma=True)
        ctx.WD = Qw(stack, nc, nc.gpsimd, "sWD", dma=True, fence=True)
        ctx.SD = Qw(stack, nc, nc.scalar, "sSD", dma=True)

        ctx.PB = [stack.enter_context(nc.psum_tensor(f"PB{i}", [128, 512], F32)) for i in range(6)]
        ctx.T0 = stack.enter_context(nc.psum_tensor("PT0", [128, 1024], BF16))
        ctx.T1 = stack.enter_context(nc.psum_tensor("PT1", [128, 1024], BF16))
        ctx.tbank_free = [None, None]

        ctx.ident = ctx.sb("ident", [128, 128], BF16)
        ctx.tri01 = ctx.sb("tri01", [128, 128], BF16)
        ctx.ones1_bf = ctx.sb("ones1b", [1, 128], BF16)
        ctx.ones1_f32 = ctx.sb("ones1f", [1, 128], F32)
        ctx.ones128_bf = ctx.sb("ones128b", [128, 128], BF16)
        ctx.blk4_bf = ctx.sb("blk4", [128, 4], BF16)
        ctx.e4_bf = ctx.sb("e4", [4, 128], BF16)
        ctx.eps128 = ctx.sb("eps128", [128, 1], F32)
        ctx.cm2 = ctx.sb("cm2", [128, 1], F32)
        ctx.cm6 = ctx.sb("cm6", [128, 1], F32)

        emit_const(ctx, "ident", np.eye(128, dtype=BF), ctx.ident)
        emit_const(ctx, "tri01", _tri01(), ctx.tri01)
        blk4 = np.zeros((128, 4), dtype=BF)
        for p in range(128):
            blk4[p, p // 32] = 1.0
        e4 = np.zeros((4, 128), dtype=BF)
        for p in range(128):
            e4[p // 32, p] = 1.0
        emit_const(ctx, "blk4", blk4, ctx.blk4_bf)
        tok = emit_const(ctx, "e4", e4, ctx.e4_bf)
        ctx.V.wait(tok)
        for tile, val in ((ctx.ones1_bf, 1.0), (ctx.ones1_f32, 1.0), (ctx.ones128_bf, 1.0),
                          (ctx.eps128, EPS), (ctx.cm2, -2.0), (ctx.cm6, -6.0)):
            ctx.V.mark(nc.vector.memset(tile[:], val))

        adbg = None
        if dbg and upto == "att":
            adbg = {"esc0": nc.dram_tensor("d_esc0", [128, 512], F32, kind="ExternalOutput").ap(),
                    "pt0": nc.dram_tensor("d_pt0", [128, 512], BF16, kind="ExternalOutput").ap(),
                    "rcb0": nc.dram_tensor("d_rcb0", [128, 512], F32, kind="ExternalOutput").ap()}
        if dbg and upto in ("A0", "qk"):
            adbg = {"stop": upto,
                    "hT": nc.dram_tensor("d_hT", [128, NCH * S], BF16, kind="ExternalOutput").ap(),
                    "QT": nc.dram_tensor("d_QT", [128, S], BF16, kind="ExternalOutput").ap(),
                    "KT": nc.dram_tensor("d_KT", [128, S], BF16, kind="ExternalOutput").ap(),
                    "Vtm": nc.dram_tensor("d_Vtm", [128, S], BF16, kind="ExternalOutput").ap(),
                    "sm": nc.dram_tensor("d_sm", [128, 64], F32, kind="ExternalOutput").ap()}
        with nc.named_scope("phaseA"):
            emit_phaseA(ctx, x_in, wq, wk, wv, at, adbg=adbg, wo=wo, wo_bf=wo_bf)
        if upto == "att":
            upto = "A"
        if upto in ("A0", "qk"):
            nc.sync.wait_ge(ctx.SD.sem, ctx.SD.n)
            nc.scalar.wait_ge(ctx.SD.sem, ctx.SD.n)
            return nc
        at_tok = ctx.SD.tok()
        if upto != "A":
            with nc.named_scope("wo"):
                emit_wo(ctx, at, wo_bf, yF, at_tok, fq, fsc, wqd)
        if upto not in ("A", "wo"):
            with nc.named_scope("phaseB"):
                emit_phaseB(ctx, (x_in, xB), yF, wqd, yB)
        if upto not in ("A", "wo", "B"):
            with nc.named_scope("phaseC"):
                emit_phaseC(ctx, (xB, xC), yB, cm, cw, cb, yF)
        if upto not in ("A", "wo", "B", "C"):
            with nc.named_scope("phaseD"):
                emit_phaseD(ctx, (xC, xB), yF, dw, db, dgw, dics, docs, yF, out)

        nc.sync.wait_ge(ctx.SD.sem, ctx.SD.n)
        nc.scalar.wait_ge(ctx.SD.sem, ctx.SD.n)
    return nc


_PROG_CACHE = {}


def kernel(**inputs):
    for _dead, _shape in (("L0_attn_kq1", (32, 128, 128)), ("L0_attn_kq2", (32, 128, 64)),
                          ("L0_attn_kk1", (32, 128, 128)), ("L0_attn_kk2", (32, 128, 64)),
                          ("L0_attn_scalingD", (1, 32, 1, 64)), ("L0_attn_scalingD2", (1, 32, 1, 64)),
                          ("L0_attn_interaction_k", (32, 64, 64))):
        assert tuple(np.asarray(inputs[_dead]).shape) == _shape
    x = np.ascontiguousarray(np.asarray(inputs["L0_attn_hidden_states"], dtype=np.float32))
    B = x.shape[0]
    base = {
        "wq": np.ascontiguousarray(np.asarray(inputs["L0_attn_wq"], dtype=np.float32)),
        "wk": np.ascontiguousarray(np.asarray(inputs["L0_attn_wk"], dtype=np.float32)),
        "wv": np.ascontiguousarray(np.asarray(inputs["L0_attn_wv"], dtype=np.float32)),
        "wo": np.ascontiguousarray(np.asarray(inputs["L0_attn_wo"], dtype=np.float32)),
        "fq": np.ascontiguousarray(np.asarray(inputs["L0_ffn_weight_q"], dtype=np.int32)),
        "fsc": np.ascontiguousarray(np.asarray(inputs["L0_ffn_weight_scales"], dtype=np.float32)),
        "cm": np.ascontiguousarray(np.asarray(inputs["L1_attn_mask"], dtype=np.int32)),
        "cw": np.ascontiguousarray(np.asarray(inputs["L1_attn_weight"], dtype=np.float32)),
        "cb": np.ascontiguousarray(np.asarray(inputs["L1_attn_bias"], dtype=np.float32)),
        "dw": np.ascontiguousarray(np.asarray(inputs["L1_ffn_weight"], dtype=np.float32)),
        "db": np.ascontiguousarray(np.asarray(inputs["L1_ffn_bias"], dtype=np.float32)),
        "dgw": np.ascontiguousarray(np.asarray(inputs["L1_ffn_gate_w"], dtype=np.float32)),
        "dics": np.ascontiguousarray(np.asarray(inputs["L1_ffn_in_channel_scale"], dtype=np.float32)),
        "docs": np.ascontiguousarray(np.asarray(inputs["L1_ffn_out_channel_scale"], dtype=np.float32)),
    }
    if "prog" not in _PROG_CACHE:
        _PROG_CACHE["prog"] = build_program()
    nc = _PROG_CACHE["prog"]
    in_maps = []
    for b in range(B):
        m = dict(base)
        m["x"] = np.ascontiguousarray(x[b])
        in_maps.append(m)
    res = run_bass_kernel_spmd(nc, in_maps, core_ids=list(range(B)))
    return np.stack([r["out"] for r in res.results], axis=0)


# version 14
# speedup vs baseline: 1.0694x; 1.0694x over previous
import math
from contextlib import ExitStack

import numpy as np
import ml_dtypes

import concourse.bass as bass
import concourse.mybir as mybir
from concourse.bass_utils import run_bass_kernel_spmd

F32 = mybir.dt.float32
BF16 = mybir.dt.bfloat16
I32 = mybir.dt.int32
AF = mybir.ActivationFunctionType
ALU = mybir.AluOpType
AX = mybir.AxisListType
BF = ml_dtypes.bfloat16

S = 2048
D = 4096
H = 32
HD = 128
NT = S // 128
NCH = D // 128
EPS = 1e-5
ISQ = 1.0 / math.sqrt(HD)


def _pi_perm():
    pi = []
    for q in range(4):
        pi += list(range(16 * q, 16 * q + 16)) + list(range(64 + 16 * q, 64 + 16 * q + 16))
    return np.array(pi, dtype=np.int64)

PI = _pi_perm()
SHUF_MASK = list(range(16, 32)) + list(range(0, 16))


def _rope_tables():
    inv_freq = 1.0 / 10000.0 ** (np.arange(0, HD, 2, dtype=np.float64) / HD)
    pos = np.arange(S, dtype=np.float64)
    ang = pos[None, :] * inv_freq[:, None]
    cosv, sinv = np.cos(ang), np.sin(ang)
    cos_t = np.zeros((128, S), dtype=np.float64)
    sins_t = np.zeros((128, S), dtype=np.float64)
    for p in range(128):
        d = int(PI[p])
        cos_t[p] = cosv[d % 64]
        sins_t[p] = (-1.0 if d < 64 else 1.0) * sinv[d % 64]
    return cos_t.astype(BF), sins_t.astype(BF)


def _tri01():
    k = np.arange(128)[:, None]
    q = np.arange(128)[None, :]
    return (q >= k).astype(BF)



class Qw:
    def __init__(self, stack, nc, eng, name, dma=False, fence=False):
        self.e = eng
        self.sem = stack.enter_context(nc.semaphore(name))
        self.n = 0
        self.inc = 16 if dma else 1
        self.fence = fence
        self.waited = {}

    def mark(self, ins):
        if self.fence:
            ins = ins._wait_ge(self.sem, self.n)
        ins.then_inc(self.sem, self.inc)
        self.n += self.inc
        return (self.sem, self.n)

    def wait(self, *tokens):
        for t in tokens:
            if t is None:
                continue
            sem, v = t
            k = id(sem)
            if self.waited.get(k, -1) >= v:
                continue
            self.e.wait_ge(sem, v)
            self.waited[k] = v

    def tok(self):
        return (self.sem, self.n)


class Box:
    pass


_NAME_CTR = [0]


def _uname(name):
    _NAME_CTR[0] += 1
    return f"{name}_{_NAME_CTR[0]}"


def _subctx(ctx, st):
    c = Box()
    c.__dict__.update(ctx.__dict__)
    c.sb = lambda name, shape, dt: st.enter_context(ctx.nc.sbuf_tensor(_uname(name), shape, dt))
    return c


def _ln_smalls(ctx, tag):
    t = Box()
    for f in ("sum", "ssq", "nm", "msq", "var", "rstd"):
        setattr(t, f, ctx.sb(f"ln_{f}_{tag}", [128, 1], F32))
    return t


def emit_ln(ctx, src_ap, trash_ap, tm, outs, dep, out_deps=(), trash_dep=()):
    V, A = ctx.V, ctx.A
    nc = ctx.nc
    n = src_ap.shape[-1]
    A.wait(*dep)
    A.wait(*trash_dep)
    tA = A.mark(nc.scalar.activation(out=trash_ap, in_=src_ap, func=AF.Square,
                                     accum_out=tm.ssq[:, 0:1]))
    V.wait(*dep)
    t1 = V.mark(nc.vector.tensor_reduce(out=tm.sum[:, 0:1], in_=src_ap, axis=AX.X, op=ALU.add))
    V.wait(t1)
    t2 = V.mark(nc.vector.tensor_scalar_mul(out=tm.nm[:, 0:1], in0=tm.sum[:, 0:1], scalar1=-1.0 / n))
    V.wait(t2)
    t3 = V.mark(nc.vector.tensor_tensor(out=tm.msq[:, 0:1], in0=tm.nm[:, 0:1], in1=tm.nm[:, 0:1], op=ALU.mult))
    V.wait(tA, t3)
    t4 = V.mark(nc.vector.scalar_tensor_tensor(out=tm.var[:, 0:1], in0=tm.ssq[:, 0:1],
                                               scalar=1.0 / n, in1=tm.msq[:, 0:1],
                                               op0=ALU.mult, op1=ALU.subtract))
    A.wait(t4)
    t5 = A.mark(nc.scalar.activation(out=tm.var[:, 0:1], in_=tm.var[:, 0:1], func=AF.Ln,
                                     bias=ctx.eps128[:, 0:1], scale=1.0))
    A.wait(t5)
    t6 = A.mark(nc.scalar.activation(out=tm.rstd[:, 0:1], in_=tm.var[:, 0:1], func=AF.Exp,
                                     scale=-0.5))
    toks = []
    for out_ap in outs:
        V.wait(t6, *out_deps)
        toks.append(V.mark(nc.vector.tensor_scalar(out=out_ap, in0=src_ap,
                                                   scalar1=tm.nm[:, 0:1], scalar2=tm.rstd[:, 0:1],
                                                   op0=ALU.add, op1=ALU.mult)))
    return toks


def emit_tr32(ctx, src_fn, dst, dst_off, dep, n_chunks=32, only_t0=False):
    nc = ctx.nc
    last = None
    ngrp = (n_chunks + 7) // 8
    for grp in range(ngrp):
        bi = 0 if only_t0 else grp % 2
        bank = (ctx.T0, ctx.T1)[bi]
        ctx.T.wait(ctx.tbank_free[bi])
        if grp == 0:
            ctx.T.wait(*dep)
        ins = None
        cnt = min(8, n_chunks - grp * 8)
        for j in range(cnt):
            dc = grp * 8 + j
            ins = nc.tensor.transpose(bank[:, j * 128:(j + 1) * 128], src_fn(dc), ctx.ident[:])
        tpe = ctx.T.mark(ins)
        ctx.V.wait(tpe)
        offs = [dst_off(grp * 8 + j) for j in range(cnt)]
        if cnt == 1:
            out_ap = dst[:, offs[0]:offs[0] + 128]
            src_ap = bank[:, 0:128]
        else:
            step = offs[1] - offs[0]
            if step == 128:
                out_ap = dst[:, offs[0]:offs[0] + cnt * 128].rearrange("p (c t) -> p c t", c=cnt)
            else:
                rem = offs[0] % step
                out_ap = dst[:].rearrange("p (c t) -> p c t", t=step)[
                    :, offs[0] // step:offs[0] // step + cnt, rem:rem + 128]
            src_ap = bank[:, 0:cnt * 128].rearrange("p (c t) -> p c t", c=cnt)
        last = ctx.V.mark(nc.vector.tensor_copy(out=out_ap, in_=src_ap))
        ctx.tbank_free[bi] = last
    return last


def phase_barrier(ctx):
    qs = (ctx.T, ctx.V, ctx.A, ctx.G, ctx.LD, ctx.WD, ctx.SD)
    toks = [q.tok() for q in qs]
    for q in qs:
        q.wait(*toks)


def emit_const(ctx, name, arr, tile):
    nc = ctx.nc
    if arr.dtype == BF:
        t = nc.inline_tensor(np.ascontiguousarray(arr).view(np.uint16), name="c_" + name)
        return ctx.LD.mark(nc.sync.dma_start(out=tile[:].bitcast(mybir.dt.uint16), in_=t.ap()))
    t = nc.inline_tensor(np.ascontiguousarray(arr), name="c_" + name)
    return ctx.LD.mark(nc.sync.dma_start(out=tile[:], in_=t.ap()))



def emit_residual_pass(ctx, x_src, y_src, y_cast, x_dst, hT):
    nc = ctx.nc
    sc_ = nc.enter_named_scope("resid", False)
    barrier = ctx.SD.tok()
    with ExitStack() as st:
        c = _subctx(ctx, st)
        xt = [c.sb("rpx0", [128, D], F32), c.sb("rpx1", [128, D], F32)]
        hbf = c.sb("rph", [128, D], BF16)
        tm = [_ln_smalls(c, "rp0"), _ln_smalls(c, "rp1")]
        yt = [c.sb("rpy0", [128, D], F32), c.sb("rpy1", [128, D], F32)]
        tmy = [_ln_smalls(c, "rpy0"), _ln_smalls(c, "rpy1")]
        xfree = [None, None]
        yfree = [None, None]
        hfree = None
        for tc in range(NT):
            b = tc % 2
            rows = slice(tc * 128, (tc + 1) * 128)
            ctx.LD.wait(xfree[b], barrier)
            tx = ctx.LD.mark(nc.sync.dma_start(out=xt[b][:], in_=x_src[rows, :]))
            ldq = ctx.WD if y_cast else ctx.LD
            dma_eng = nc.gpsimd if y_cast else nc.sync
            ldq.wait(yfree[b], barrier)
            ty = ldq.mark(dma_eng.dma_start(out=yt[b][:], in_=y_src[rows, :]))
            t_ln = emit_ln(ctx, yt[b][:], hbf[:], tmy[b], [yt[b][:]], [ty],
                           out_deps=(), trash_dep=[hfree])[0]
            ctx.G.wait(t_ln, tx)
            tadd = ctx.G.mark(nc.gpsimd.tensor_tensor(out=yt[b][:], in0=yt[b][:], in1=xt[b][:], op=ALU.add))
            xfree[b] = tadd
            ctx.SD.wait(tadd)
            tst = ctx.SD.mark(nc.scalar.dma_start(out=x_dst[rows, :], in_=yt[b][:]))
            yfree[b] = tst
            if hT is not None:
                t_h = emit_ln(ctx, yt[b][:], hbf[:], tm[b], [hbf[:]], [tadd],
                              out_deps=[hfree], trash_dep=[hfree])[0]
                hfree = emit_tr32(ctx, lambda dc: hbf[:, dc * 128:(dc + 1) * 128],
                                  hT, lambda dc, tc=tc: dc * S + tc * 128, [t_h])
                yfree[b] = hfree
    phase_barrier(ctx)
    nc.leave_named_scope("resid", sc_[0], False)
    return ctx.SD.tok()



def emit_phaseA(ctx, x_src, wq, wk, wv, at_dram, adbg=None, wo=None, wo_bf=None):
    nc = ctx.nc
    T, V, A_, G, LD, WD, SD = ctx.T, ctx.V, ctx.A, ctx.G, ctx.LD, ctx.WD, ctx.SD
    B0, B1, B2, B3, B4, B5 = ctx.PB
    cos_np, sins_np = _rope_tables()
    with ExitStack() as st:
        c = _subctx(ctx, st)
        hT = c.sb("A_hT", [128, NCH * S], BF16)
        cosT = c.sb("A_cos", [128, S], BF16)
        sinT = c.sb("A_sin", [128, S], BF16)
        QT = [c.sb("A_QT0", [128, S], BF16), c.sb("A_QT1", [128, S], BF16)]
        KT = [c.sb("A_KT0", [128, S], BF16), c.sb("A_KT1", [128, S], BF16)]
        Vtm = [c.sb("A_Vtm0", [128, S], BF16), c.sb("A_Vtm1", [128, S], BF16)]
        scrA = c.sb("A_scr", [128, S], BF16)
        PT = [c.sb(f"A_PT{i}", [128, 512], BF16) for i in range(3)]
        fb = [c.sb(f"A_f{i}", [128, 512], F32) for i in range(2)]
        tl = [c.sb(f"A_tl{i}", [128, 512], F32) for i in range(2)]
        ATs = [c.sb("A_ATs0", [128, 512], BF16), c.sb("A_ATs1", [128, 512], BF16)]
        sm = c.sb("A_sm", [128, 64], F32)

        emit_const(ctx, "cosT", cos_np, cosT)
        tk = emit_const(ctx, "sinT", sins_np, sinT)
        V.wait(tk)
        t_ms = V.mark(nc.vector.memset(sm[0:1, 20:21], math.log(ISQ)))

        scA0 = nc.enter_named_scope("A0", False)
        with ExitStack() as st0:
            c0 = _subctx(ctx, st0)
            xt = c0.sb("A0_x", [128, D], F32)
            hbf = c0.sb("A0_h", [128, D], BF16)
            tm0 = _ln_smalls(c0, "A0")
            xfree = None
            hfree = None
            for tc in range(NT):
                LD.wait(xfree)
                tx = LD.mark(nc.sync.dma_start(out=xt[:], in_=x_src[tc * 128:(tc + 1) * 128, :]))
                t_h = emit_ln(ctx, xt[:], hbf[:], tm0, [hbf[:]], [tx],
                              out_deps=[hfree], trash_dep=[hfree])[0]
                hfree = emit_tr32(ctx, lambda dc: hbf[:, dc * 128:(dc + 1) * 128],
                                  hT, lambda dc, tc=tc: dc * S + tc * 128, [t_h])
                xfree = t_h

        phase_barrier(ctx)
        nc.leave_named_scope("A0", scA0[0], False)
        if adbg is not None and "hT" in adbg:
            ctx.SD.wait(hfree)
            ctx.SD.mark(nc.scalar.dma_start(out=adbg["hT"], in_=hT[:]))
            if adbg.get("stop") == "A0":
                return
        st_w = st.enter_context(ExitStack())
        cw_ = _subctx(ctx, st_w)
        slot = [cw_.sb(f"A_w{i}", [128, D], BF16) for i in range(3)]
        wraw = cw_.sb("A_wraw", [128, 2048], BF16)
        slot_ready = [None] * 3
        slot_free = [None] * 3
        state = Box()
        state.wraw_free = None

        def load_tensor(idx):
            h_, t_ = divmod(idx, 3)
            w = (wq, wk, wv)[t_]
            s_ = idx % 3
            cols = slice(h_ * 128, (h_ + 1) * 128)
            if t_ == 0 and wo_bf is not None:
                rows = slice(h_ * 128, (h_ + 1) * 128)
                for cc8 in range(8):
                    WD.mark(nc.gpsimd.dma_start(
                        out=wo_bf[rows, cc8 * 512:(cc8 + 1) * 512],
                        in_=wo[rows, cc8 * 512:(cc8 + 1) * 512]))
            if t_ == 2:
                WD.wait(slot_free[s_])
                tkk = None
                for part in range(2):
                    tkk = WD.mark(nc.gpsimd.dma_start(
                        out=slot[s_][:].rearrange("p (cc m) -> p cc m", cc=NCH)[:, part * 16:(part + 1) * 16, :],
                        in_=w[:, cols].rearrange("(cc p) m -> p cc m", p=128)[:, part * 16:(part + 1) * 16, :]))
                slot_ready[s_] = tkk
            else:
                tp = None
                for part in range(2):
                    WD.wait(state.wraw_free)
                    tkk = WD.mark(nc.gpsimd.dma_start(
                        out=wraw[:].rearrange("p (cc m) -> p cc m", cc=16),
                        in_=w[:, cols].rearrange("(cc p) m -> p cc m", p=128)[:, part * 16:(part + 1) * 16, :]))
                    G.wait(tkk)
                    G.wait(slot_free[s_])
                    tp = G.mark(nc.gpsimd.tensor_copy(
                        out=slot[s_][:, part * 2048:(part + 1) * 2048].rearrange(
                            "p (cc b hh i) -> p cc b hh i", cc=16, b=4, hh=2, i=16),
                        in_=wraw[:].rearrange("p (cc hh b i) -> p cc b hh i", cc=16, hh=2, b=4, i=16)))
                    state.wraw_free = tp
                slot_ready[s_] = tp

        for idx in range(3):
            load_tensor(idx)

        ps_free = {0: None, 1: None, 2: None}
        pt_free = [None, None, None]
        psS_free = [None, None]
        psO_free = None
        psSum_free = None
        tl_free = [None, None]
        rope_fb = [None, None]
        ATs_free = [None, None]
        scr_read_tok = [None]
        sum_bank = ctx.T1[:, 0:1024].bitcast(F32)
        V.wait(t_ms)
        t_nb = V.mark(nc.vector.memset(sm[:, 33:34], -50.0))

        def prep_gen(hh):
            par = hh % 2
            QTp, KTp, Vtmp = QT[par], KT[par], Vtm[par]
            stt_ = Box()
            stt_.qt_tok = stt_.kt_tok = stt_.vtm_tok = None
            vcopy_tok = None
            for g in range(4):
                gcol = slice(g * 512, (g + 1) * 512)
                proj = {}
                for t_, ps in ((0, B0), (1, B1), (2, B2)):
                    s_ = t_
                    T.wait(slot_ready[s_], ps_free[t_])
                    ins = None
                    for dc in range(NCH):
                        ins = nc.tensor.matmul(ps[:], slot[s_][:, dc * 128:(dc + 1) * 128],
                                               hT[:, dc * S + g * 512: dc * S + (g + 1) * 512],
                                               start=(dc == 0), stop=(dc == NCH - 1))
                    proj[t_] = T.mark(ins)
                    if g == 3:
                        slot_free[s_] = proj[t_]
                        if hh + 1 < H:
                            load_tensor(3 * (hh + 1) + t_)
                    yield
                for t_, ps, dstT in ((0, B0, QTp), (1, B1, KTp)):
                    V.wait(proj[t_], rope_fb[0])
                    tsq_ = V.mark(nc.vector.stream_shuffle(out=fb[0][:], in_=ps[:], mask=SHUF_MASK))
                    V.wait(rope_fb[1])
                    tcq_ = V.mark(nc.vector.tensor_tensor(out=fb[1][:], in0=ps[:], in1=cosT[:, gcol], op=ALU.mult))
                    ps_free[t_] = tcq_
                    G.wait(tsq_)
                    tu_ = G.mark(nc.gpsimd.tensor_tensor(out=fb[0][:], in0=fb[0][:], in1=sinT[:, gcol], op=ALU.mult))
                    V.wait(tu_, tcq_)
                    tok = V.mark(nc.vector.tensor_tensor(out=dstT[:, gcol], in0=fb[1][:], in1=fb[0][:], op=ALU.add))
                    rope_fb[0] = tok
                    rope_fb[1] = tok
                    if t_ == 0:
                        stt_.qt_tok = tok
                    else:
                        stt_.kt_tok = tok
                V.wait(proj[2], scr_read_tok[0])
                vcopy_tok = V.mark(nc.vector.tensor_copy(out=scrA[:, gcol], in_=B2[:]))
                ps_free[2] = vcopy_tok
                yield
            stt_.vtm_tok = emit_tr32(ctx, lambda kc: scrA[:, kc * 128:(kc + 1) * 128],
                                     Vtmp, lambda kc: kc * 128, [vcopy_tok], n_chunks=16,
                                     only_t0=True)
            scr_read_tok[0] = stt_.vtm_tok
            yield stt_

        def attn_gen(hh, stp):
            nonlocal psO_free, psSum_free
            par = hh % 2
            QTp, KTp, Vtmp = QT[par], KT[par], Vtm[par]
            for g in range(4):
                gcol = slice(g * 512, (g + 1) * 512)
                Kg = 4 * (g + 1)
                pv_last = None
                pt_toks = [None] * Kg
                for kt in range(Kg + 2):
                    if kt < Kg:
                        bs = kt % 2
                        b3 = kt % 3
                        psS = (B3, B5)[bs]
                        j = kt - 4 * g
                        off = 128 * max(j, 0)
                        T.wait(psS_free[bs], stp.qt_tok, stp.kt_tok)
                        tS = T.mark(nc.tensor.matmul(psS[:, off:512], KTp[:, kt * 128:(kt + 1) * 128],
                                                     QTp[:, g * 512 + off:(g + 1) * 512],
                                                     start=True, stop=True))
                        A_.wait(tS, t_nb, pt_free[b3])
                        tX = A_.mark(nc.scalar.activation(out=PT[b3][:, off:512], in_=psS[:, off:512],
                                                          func=AF.Exp, bias=sm[:, 33:34], scale=ISQ))
                        psS_free[bs] = tX
                        if j >= 0:
                            V.wait(tX)
                            tM = V.mark(nc.vector.tensor_tensor(out=PT[b3][:, off:off + 128],
                                                                in0=PT[b3][:, off:off + 128],
                                                                in1=ctx.tri01[:], op=ALU.mult))
                            pt_toks[kt] = tM
                        else:
                            pt_toks[kt] = tX
                        if adbg is not None and hh == 0 and g == 0 and kt == 0 and "esc0" in adbg:
                            ctx.SD.wait(pt_toks[kt])
                            ctx.SD.mark(nc.scalar.dma_start(out=adbg["pt0"], in_=PT[b3][:]))
                    if kt >= 2:
                        kp = kt - 2
                        bp = kp % 3
                        jp = kp - 4 * g
                        offp = 128 * max(jp, 0)
                        T.wait(pt_toks[kp])
                        if kp == 0:
                            T.wait(psO_free, psSum_free)
                        nc.tensor.matmul(B4[:, offp:512], Vtmp[:, kp * 128:(kp + 1) * 128],
                                         PT[bp][:, offp:512],
                                         start=(kp == 0), stop=(kp == Kg - 1))
                        ins = nc.tensor.matmul(sum_bank[:, offp:512], ctx.ones128_bf[:],
                                               PT[bp][:, offp:512],
                                               start=(kp == 0), stop=(kp == Kg - 1))
                        pv_last = T.mark(ins)
                        pt_free[bp] = pv_last
                    yield
                tp_ = g % 2
                V.wait(pv_last, tl_free[tp_])
                tR = V.mark(nc.vector.reciprocal(out=tl[tp_][:], in_=sum_bank))
                psSum_free = tR
                yield
                V.wait(ATs_free[tp_])
                tAT = V.mark(nc.vector.tensor_tensor(out=ATs[tp_][:], in0=B4[:], in1=tl[tp_][:], op=ALU.mult))
                psO_free = tAT
                tl_free[tp_] = tAT
                if adbg is not None and hh == 0 and g == 0 and "rcb0" in adbg:
                    ctx.SD.wait(tR)
                    ctx.SD.mark(nc.scalar.dma_start(out=adbg["rcb0"], in_=tl[0][:]))
                SD.wait(tAT)
                ATs_free[tp_] = SD.mark(nc.scalar.dma_start(
                    out=at_dram[hh * 128:(hh + 1) * 128, g * 512:(g + 1) * 512], in_=ATs[tp_][:]))
                yield

        def run_gen(gen):
            r = None
            for r in gen:
                pass
            return r

        stp = run_gen(prep_gen(0))
        if adbg is not None and "QT" in adbg:
            ctx.SD.wait(stp.qt_tok, stp.kt_tok, stp.vtm_tok)
            ctx.SD.mark(nc.scalar.dma_start(out=adbg["QT"], in_=QT[0][:]))
            ctx.SD.mark(nc.scalar.dma_start(out=adbg["KT"], in_=KT[0][:]))
            ctx.SD.mark(nc.scalar.dma_start(out=adbg["Vtm"], in_=Vtm[0][:]))
            ctx.SD.mark(nc.scalar.dma_start(out=adbg["sm"], in_=sm[:]))
            if adbg.get("stop") == "qk":
                return
        for h in range(H):
            ag = attn_gen(h, stp)
            pg = prep_gen(h + 1) if h + 1 < H else None
            stn = None
            alive_a = True
            while alive_a or pg is not None:
                if pg is not None:
                    try:
                        r = next(pg)
                        if r is not None:
                            stn = r
                            pg = None
                    except StopIteration:
                        pg = None
                for _ in range(4):
                    if alive_a:
                        try:
                            next(ag)
                        except StopIteration:
                            alive_a = False
            stp = stn
    phase_barrier(ctx)
    return ctx.SD.tok()


def esc_of(fb, bs):
    return fb[2 + bs]



def emit_wo(ctx, at_dram, wo_bf, y_dram, at_ready, wq_int, wscales, wqd):
    nc = ctx.nc
    T, V, A_, G, LD, WD, SD = ctx.T, ctx.V, ctx.A, ctx.G, ctx.LD, ctx.WD, ctx.SD
    rd_guard = (ctx.LD.tok(), ctx.WD.tok())
    B0, B1 = ctx.PB[0], ctx.PB[1]
    SW = 256
    NSL = D // SW
    with ExitStack() as st:
        c = _subctx(ctx, st)
        ATb = c.sb("WO_at", [128, NCH * S], BF16)
        wob = [c.sb("WO_w0", [128, NCH * 512], BF16)]
        ysb = [c.sb("WO_y0", [128, 512], F32), c.sb("WO_y1", [128, 512], F32)]
        qt = [c.sb(f"WO_qt{i}", [128, 512], F32) for i in range(2)]
        sct = [c.sb(f"WO_sct{i}", [128, 256], F32) for i in range(2)]
        wbs = c.sb("WO_wbs", [128, 2048], BF16)
        dq2 = [[c.sb(f"WO_dq{k}_{i}", [128, SW], F32) for i in range(4)] for k in range(4)]
        one128f = c.sb("WO_one", [128, 1], F32)
        ctx.V.mark(nc.vector.memset(one128f[:], 1.0))

        LD.wait(at_ready)
        t_at = None
        for part in range(8):
            t_at = LD.mark(nc.sync.dma_start(
                out=ATb[:].rearrange("p (cc t) -> p cc t", cc=NCH)[:, part * 4:(part + 1) * 4, :],
                in_=at_dram.rearrange("(cc p) t -> p cc t", p=128)[:, part * 4:(part + 1) * 4, :]))
        w_ready = [None]
        w_free = [None]
        ev_free = [None, None]
        st_free = [None, None]

        def load_wo(oc):
            b = 0
            LD.wait(w_free[b])
            tk = None
            for part in range(4):
                tk = LD.mark(nc.sync.dma_start(
                    out=wob[b][:].rearrange("p (cc n) -> p cc n", cc=NCH)[:, part * 8:(part + 1) * 8, :],
                    in_=wo_bf[:, oc * 512:(oc + 1) * 512].rearrange("(cc p) n -> p cc n", p=128)[:, part * 8:(part + 1) * 8, :]))
            w_ready[b] = tk

        def wo_gen():
            load_wo(0)
            for oc in range(8):
                b = 0
                last_mm = None
                for tc in range(NT):
                    pb = tc % 2
                    ps = (B0, B1)[pb]
                    T.wait(w_ready[b], t_at, ev_free[pb])
                    ins = None
                    for dc in range(NCH):
                        ins = nc.tensor.matmul(ps[:], ATb[:, dc * S + tc * 128: dc * S + (tc + 1) * 128],
                                               wob[b][:, dc * 512:(dc + 1) * 512],
                                               start=(dc == 0), stop=(dc == NCH - 1))
                    tmm = T.mark(ins)
                    last_mm = tmm
                    V.wait(tmm, st_free[pb])
                    tev = V.mark(nc.vector.tensor_copy(out=ysb[pb][:], in_=ps[:]))
                    ev_free[pb] = tev
                    SD.wait(tev, *rd_guard)
                    st_free[pb] = SD.mark(nc.scalar.dma_start(
                        out=y_dram[tc * 128:(tc + 1) * 128, oc * 512:(oc + 1) * 512], in_=ysb[pb][:]))
                    yield
                w_free[b] = last_mm
                if oc + 1 < 8:
                    load_wo(oc + 1)

        def dq_gen():
            NS = 512
            tj = [None] * NS
            t4a = [None] * NS
            t5a = [None] * NS
            tga = [None] * NS
            tta = [None] * NS
            tua = [None] * NS
            twa = [None] * NS
            t10a = [None] * NS
            qtok = {}
            tscs = [None] * 32
            wbs_store = None

            def qsrc(B):
                rc, b = divmod(B, 8)
                return wq_int[rc * 128:(rc + 1) * 128, b * 512:(b + 1) * 512]

            for B in (0, 1):
                qtok[B] = WD.mark(nc.gpsimd.dma_start(out=qt[B % 2][:], in_=qsrc(B)))
            techo = WD.mark(nc.gpsimd.dma_start(
                out=qt[0][:, 508:512], in_=wq_int[0:128, 508:512]))
            tscs[0] = LD.mark(nc.sync.dma_start(out=sct[0][:], in_=wscales[0:128, :]))
            for i in range(NS + 4):
                p = i - 4
                if p >= 0:
                    rcp, slp = divmod(p, 16)
                    colq = slice((slp % 8) * SW, (slp % 8 + 1) * SW)
                    jb = dq2[p % 4][1]
                    V.wait(twa[p], tscs[rcp], wbs_store)
                    t10a[p] = V.mark(nc.vector.tensor_tensor(
                        out=wbs[:, colq].rearrange("p (a b) -> p a b", b=16),
                        in0=jb[:].rearrange("p (a b) -> p a b", b=16),
                        in1=sct[rcp % 2][:, slp * 16:(slp + 1) * 16].unsqueeze(2).broadcast_to([128, 16, 16]),
                        op=ALU.mult))
                    if slp == 7 or slp == 15:
                        SD.wait(t10a[p])
                        half = slice(0, 2048) if slp == 7 else slice(2048, 4096)
                        wbs_store = SD.mark(nc.scalar.dma_start(
                            out=wqd[rcp * 128:(rcp + 1) * 128, half], in_=wbs[:]))
                if i < NS:
                    B = i // 2
                    qb = qt[B % 2]
                    qoff = (i % 2) * 256
                    sb_, jb, r1, r2 = dq2[i % 4]
                    V.wait(qtok[B], techo if i < 2 else None,
                           twa[i - 4] if i >= 4 else None)
                    t2s = V.mark(nc.vector.tensor_single_scalar(
                        out=sb_[:], in_=qb[:, qoff:qoff + 256], scalar=8.0, op=ALU.is_ge))
                    V.wait(t2s)
                    tj[i] = V.mark(nc.vector.scalar_tensor_tensor(
                        out=jb[:], in0=sb_[:], scalar=-8.0,
                        in1=qb[:, qoff:qoff + 256], op0=ALU.mult, op1=ALU.add))
                    if i % 2 == 1 and B + 2 < 256:
                        WD.wait(tj[i])
                        qtok[B + 2] = WD.mark(nc.gpsimd.dma_start(out=qt[B % 2][:], in_=qsrc(B + 2)))
                    rci = i // 16
                    if i % 16 == 8 and rci + 1 < 32:
                        LD.wait(t10a[(rci - 1) * 16 + 15] if rci >= 1 else None)
                        tscs[rci + 1] = LD.mark(nc.sync.dma_start(
                            out=sct[(rci + 1) % 2][:], in_=wscales[(rci + 1) * 128:(rci + 2) * 128, :]))
                q2 = i - 2
                if 0 <= q2 < NS:
                    sb_, jb, r1, r2 = dq2[q2 % 4]
                    V.wait(t4a[q2])
                    tta[q2] = V.mark(nc.vector.scalar_tensor_tensor(
                        out=r1[:], in0=jb[:], scalar=0.5, in1=r1[:], op0=ALU.mult, op1=ALU.add))
                a = i - 1
                if 0 <= a < NS:
                    sb_, jb, r1, r2 = dq2[a % 4]
                    A_.wait(tj[a], tua[a - 4] if a >= 4 else None)
                    t4a[a] = A_.mark(nc.scalar.activation(out=r1[:], in_=jb[:], func=AF.Relu,
                                                          bias=ctx.cm2[:, 0:1], scale=0.5))
                    tga[a] = A_.mark(nc.scalar.activation(out=sb_[:], in_=sb_[:], func=AF.Sign,
                                                          bias=one128f[:, 0:1], scale=-2.0))
                    t5a[a] = A_.mark(nc.scalar.activation(out=r2[:], in_=jb[:], func=AF.Relu,
                                                          bias=ctx.cm6[:, 0:1], scale=1.0))
                g3 = i - 3
                if 0 <= g3 < NS:
                    sb_, jb, r1, r2 = dq2[g3 % 4]
                    G.wait(tta[g3], t5a[g3], t10a[g3 - 4] if g3 >= 4 else None)
                    tua[g3] = G.mark(nc.gpsimd.tensor_tensor(out=jb[:], in0=r1[:], in1=r2[:], op=ALU.add))
                    V.wait(tua[g3], tga[g3])
                    twa[g3] = V.mark(nc.vector.tensor_tensor(out=jb[:], in0=jb[:], in1=sb_[:], op=ALU.mult))
                yield

        wg, dg = wo_gen(), dq_gen()
        alive_w = alive_d = True
        while alive_w or alive_d:
            if alive_w:
                try:
                    next(wg)
                except StopIteration:
                    alive_w = False
            for _ in range(4):
                if alive_d:
                    try:
                        next(dg)
                    except StopIteration:
                        alive_d = False
    phase_barrier(ctx)
    return ctx.SD.tok()


def emit_phaseB(ctx, x_src, y_prev, wqd, y_dram):
    nc = ctx.nc
    T, V, A_, G, LD, WD, SD = ctx.T, ctx.V, ctx.A, ctx.G, ctx.LD, ctx.WD, ctx.SD
    B0, B1 = ctx.PB[0], ctx.PB[1]
    with ExitStack() as st:
        c = _subctx(ctx, st)
        hT = c.sb("B_hT", [128, NCH * S], BF16)
        emit_residual_pass(ctx, x_src[0], y_prev, False, x_src[1], hT)
        rd_guard = (ctx.LD.tok(), ctx.WD.tok())

        WT = c.sb("B_WT", [128, NCH * 512], BF16)
        wbf = [c.sb("B_w0", [128, D], BF16), c.sb("B_w1", [128, D], BF16)]
        ysb = [c.sb("B_y0", [128, 512], BF16), c.sb("B_y1", [128, 512], BF16)]
        wt_free = None
        ev_free = [None, None]
        st_free = [None, None]
        w_free = [None, None]
        for oc in range(8):
            tr_tok = None
            for sc_ in range(4):
                wb = (oc * 4 + sc_) % 2
                row0 = oc * 512 + sc_ * 128
                LD.wait(w_free[wb])
                tw = LD.mark(nc.sync.dma_start(out=wbf[wb][:], in_=wqd[row0:row0 + 128, :]))
                tr_tok = emit_tr32(ctx, lambda ic, wb=wb: wbf[wb][:, ic * 128:(ic + 1) * 128],
                                   WT, lambda ic, sc_=sc_: ic * 512 + sc_ * 128,
                                   [tw, wt_free], n_chunks=32)
                w_free[wb] = tr_tok
            last_mm = None
            for tc in range(NT):
                pb = tc % 2
                ps = (B0, B1)[pb]
                T.wait(tr_tok, ev_free[pb])
                ins = None
                for dc in range(NCH):
                    ins = nc.tensor.matmul(ps[:], hT[:, dc * S + tc * 128: dc * S + (tc + 1) * 128],
                                           WT[:, dc * 512:(dc + 1) * 512],
                                           start=(dc == 0), stop=(dc == NCH - 1))
                tmm = T.mark(ins)
                last_mm = tmm
                A_.wait(tmm, st_free[pb])
                tev = A_.mark(nc.scalar.activation(out=ysb[pb][:], in_=ps[:], func=AF.Copy, scale=1.0))
                ev_free[pb] = tev
                SD.wait(tev, *rd_guard)
                st_free[pb] = SD.mark(nc.scalar.dma_start(
                    out=y_dram[tc * 128:(tc + 1) * 128, oc * 512:(oc + 1) * 512], in_=ysb[pb][:]))
            wt_free = last_mm
    phase_barrier(ctx)
    return ctx.SD.tok()



def emit_phaseC(ctx, x_src, y_prev, cmask, cw, cbias, y_dram):
    nc = ctx.nc
    T, V, A_, G, LD, WD, SD = ctx.T, ctx.V, ctx.A, ctx.G, ctx.LD, ctx.WD, ctx.SD
    B0, B1, B4, B5 = ctx.PB[0], ctx.PB[1], ctx.PB[4], ctx.PB[5]
    with ExitStack() as st:
        c = _subctx(ctx, st)
        hT = c.sb("C_hT", [128, NCH * S], BF16)
        emit_residual_pass(ctx, x_src[0], y_prev, True, x_src[1], hT)
        rd_guard = (ctx.LD.tok(), ctx.WD.tok())

        WT = c.sb("C_WT", [128, NCH * 512], BF16)
        mt = c.sb("C_mt", [128, D], I32)
        wbf = [c.sb("C_w0", [128, D], BF16), c.sb("C_w1", [128, D], BF16)]
        wsum = c.sb("C_ws", [128, 128], BF16)
        bm01 = c.sb("C_bm", [4, 128], BF16)
        cbb = c.sb("C_cb", [1, D], BF16)
        ysb = [c.sb("C_y0", [128, 512], F32), c.sb("C_y1", [128, 512], F32)]

        t_cb = WD.mark(nc.gpsimd.dma_start(out=cbb[0:1, :], in_=cbias.rearrange("(o n) -> o n", o=1)))
        mt_free = None
        ws_free = None
        bm_free = None
        wt_free = None
        psX_free = None
        psM_free = None
        ev_free = [None, None]
        st_free = [None, None]
        w_free = [None, None]
        for oc in range(8):
            tr_tok = None
            for sc_ in range(4):
                wb = (oc * 4 + sc_) % 2
                row0 = oc * 512 + sc_ * 128
                LD.wait(mt_free)
                tmk = LD.mark(nc.sync.dma_start(out=mt[:], in_=cmask[row0:row0 + 128, :]))
                WD.wait(w_free[wb])
                tw = WD.mark(nc.gpsimd.dma_start(out=wbf[wb][:], in_=cw[row0:row0 + 128, :]))
                V.wait(tmk, ws_free)
                with nc.allow_low_precision(reason="mask block-sum only tested >0"):
                    tws = V.mark(nc.vector.tensor_reduce(out=wsum[:, :],
                                                         in_=mt[:].rearrange("p (a b) -> p a b", b=32),
                                                         axis=AX.X, op=ALU.add))
                mt_free = tws
                T.wait(tws, psM_free)
                tbs = T.mark(nc.tensor.matmul(B4[0:4, 0:128], ctx.blk4_bf[:], wsum[:, :], start=True, stop=True))
                ws_free = tbs
                V.wait(tbs, bm_free)
                tbm = V.mark(nc.vector.tensor_single_scalar(out=bm01[0:4, :], in_=B4[0:4, 0:128],
                                                            scalar=0.0, op=ALU.is_gt))
                psM_free = tbm
                T.wait(tbm, psX_free)
                tex = T.mark(nc.tensor.matmul(B5[:, 0:128], ctx.e4_bf[:], bm01[0:4, :], start=True, stop=True))
                bm_free = tex
                V.wait(tex, tw)
                tmask = V.mark(nc.vector.tensor_tensor(
                    out=wbf[wb][:].rearrange("p (a b) -> p a b", b=32),
                    in0=wbf[wb][:].rearrange("p (a b) -> p a b", b=32),
                    in1=B5[:, 0:128].unsqueeze(2).broadcast_to([128, 128, 32]),
                    op=ALU.mult))
                psX_free = tmask
                tr_tok = emit_tr32(ctx, lambda ic, wb=wb: wbf[wb][:, ic * 128:(ic + 1) * 128],
                                   WT, lambda ic, sc_=sc_: ic * 512 + sc_ * 128,
                                   [tmask, wt_free], n_chunks=32)
                w_free[wb] = tr_tok
            last_mm = None
            for tc in range(NT):
                pb = tc % 2
                ps = (B0, B1)[pb]
                T.wait(tr_tok, ev_free[pb], t_cb)
                for dc in range(NCH):
                    nc.tensor.matmul(ps[:], hT[:, dc * S + tc * 128: dc * S + (tc + 1) * 128],
                                     WT[:, dc * 512:(dc + 1) * 512],
                                     start=(dc == 0), stop=False)
                ins = nc.tensor.matmul(ps[:], ctx.ones1_bf[:], cbb[0:1, oc * 512:(oc + 1) * 512],
                                       start=False, stop=True)
                tmm = T.mark(ins)
                last_mm = tmm
                A_.wait(tmm, st_free[pb])
                tev = A_.mark(nc.scalar.activation(out=ysb[pb][:], in_=ps[:], func=AF.Copy, scale=1.0))
                ev_free[pb] = tev
                SD.wait(tev, *rd_guard)
                st_free[pb] = SD.mark(nc.scalar.dma_start(
                    out=y_dram[tc * 128:(tc + 1) * 128, oc * 512:(oc + 1) * 512], in_=ysb[pb][:]))
            wt_free = last_mm
    phase_barrier(ctx)
    return ctx.SD.tok()



def emit_phaseD(ctx, x_src, y_prev, dw, dbias, dgw, dics, docs, y_dram, out_dram):
    nc = ctx.nc
    T, V, A_, G, LD, WD, SD = ctx.T, ctx.V, ctx.A, ctx.G, ctx.LD, ctx.WD, ctx.SD
    B0, B1, B2, B3, B4, B5 = ctx.PB
    with ExitStack() as st:
        c = _subctx(ctx, st)
        routT = c.sb("D_routT", [8, S], F32)
        emit_residual_pass(ctx, x_src[0], y_prev, False, x_src[1], None)
        rd_guard = (ctx.LD.tok(), ctx.WD.tok())
        with ExitStack() as stm:
            cm = _subctx(ctx, stm)
            XST = cm.sb("D_XST", [128, NCH * S], BF16)
            with ExitStack() as st1:
                c1 = _subctx(ctx, st1)
                xt = [c1.sb("D_x0", [128, D], F32)]
                hbf = c1.sb("D_h", [128, D], BF16)
                hTD = c1.sb("D_hTD", [128, D], BF16)
                gwT = c1.sb("D_gwT", [128, 256], BF16)
                ctx.identf = c1.sb("D_identf", [128, 128], F32)
                emit_const(ctx, "identf", np.eye(128, dtype=np.float32), ctx.identf)
                icsb = c1.sb("D_ics", [8, D], F32)
                rg = c1.sb("D_rg", [128, 8], F32)
                rbf = c1.sb("D_rbf", [128, 8], F32)
                smD = c1.sb("D_sm", [128, 2], F32)
                tmD = [_ln_smalls(c1, "D0"), _ln_smalls(c1, "D1")]
                gw8 = c1.sb("D_gw8", [8, D], BF16)

                t_ic = ctx.LD.mark(nc.sync.dma_start(out=icsb[0:8, :], in_=dics))
                t_gw = WD.mark(nc.gpsimd.dma_start(out=gw8[0:8, :], in_=dgw))
                ctx.T.wait(ctx.tbank_free[1], t_gw)
                ins_g = None
                for dc in range(NCH):
                    ins_g = nc.tensor.transpose(ctx.T1[:, dc * 8:(dc + 1) * 8],
                                                gw8[0:8, dc * 128:(dc + 1) * 128], ctx.ident[0:8, 0:8])
                t_gt = T.mark(ins_g)
                V.wait(t_gt)
                gwt_tok = V.mark(nc.vector.tensor_copy(out=gwT[:, 0:256], in_=ctx.T1[:, 0:256]))
                ctx.tbank_free[1] = gwt_tok
                xfree = [None, None]
                hfree = None
                hTD_free = None
                psG_free = None
                psI_free = [None, None]
                rt_free = None
                state5 = Box()
                state5.b5_free = None
                for tc in range(NT):
                    b = 0
                    rows = slice(tc * 128, (tc + 1) * 128)
                    LD.wait(xfree[b])
                    tx = LD.mark(nc.sync.dma_start(out=xt[b][:], in_=x_src[1][rows, :]))
                    t_hb = emit_ln(ctx, xt[b][:], hbf[:], tmD[b], [hbf[:]], [tx],
                                   out_deps=[hfree], trash_dep=[hfree])[0]
                    hfree = t_hb
                    xfree[b] = t_hb
                    thT = emit_tr32(ctx, lambda dc: hbf[:, dc * 128:(dc + 1) * 128],
                                    hTD, lambda dc: dc * 128, [t_hb, hTD_free])
                    hfree = thT
                    T.wait(thT, gwt_tok, psG_free)
                    ins = None
                    for dc in range(NCH):
                        ins = nc.tensor.matmul(B4[:, 0:8], hTD[:, dc * 128:(dc + 1) * 128],
                                               gwT[:, dc * 8:(dc + 1) * 8],
                                               start=(dc == 0), stop=(dc == NCH - 1))
                    t_g = T.mark(ins)
                    V.wait(t_g)
                    t_mx = V.mark(nc.vector.tensor_reduce(out=smD[:, 0:1], in_=B4[:, 0:8],
                                                          axis=AX.X, op=ALU.max, negate=True))
                    A_.wait(t_mx)
                    t_eg = A_.mark(nc.scalar.activation(out=rg[:, :], in_=B4[:, 0:8], func=AF.Exp,
                                                        bias=smD[:, 0:1], scale=1.0))
                    psG_free = t_eg
                    V.wait(t_eg)
                    t_sm = V.mark(nc.vector.tensor_reduce(out=smD[:, 1:2], in_=rg[:, :], axis=AX.X, op=ALU.add))
                    V.wait(t_sm)
                    V.e.wait_ge(t_sm[0], t_sm[1])
                    t_rc = V.mark(nc.vector.reciprocal(out=smD[:, 0:1], in_=smD[:, 1:2]))
                    V.wait(t_rc)
                    V.e.wait_ge(t_rc[0], t_rc[1])
                    t_rn = V.mark(nc.vector.tensor_scalar_mul(out=rbf[:, :], in0=rg[:, :], scalar1=smD[:, 0:1]))
                    ctx.T.wait(state5.b5_free, t_rn)
                    t_tr = T.mark(nc.tensor.transpose(ctx.PB[5][0:8, 0:128], rbf[:, :], ctx.identf[:]))
                    V.wait(t_tr, rt_free)
                    t_rt = V.mark(nc.vector.tensor_copy(out=routT[0:8, tc * 128:(tc + 1) * 128],
                                                        in_=ctx.PB[5][0:8, 0:128]))
                    state5.b5_free = t_rt
                    t_x = None
                    for dc in range(NCH):
                        pb = dc % 2
                        psI = (B2, B3)[pb]
                        T.wait(t_rt, t_ic, psI_free[pb])
                        t_i = T.mark(nc.tensor.matmul(psI[:, 0:128],
                                                      icsb[0:8, dc * 128:(dc + 1) * 128],
                                                      routT[0:8, tc * 128:(tc + 1) * 128],
                                                      start=True, stop=True))
                        V.wait(t_i, thT)
                        t_x = V.mark(nc.vector.tensor_tensor(
                            out=XST[:, dc * S + tc * 128:dc * S + (tc + 1) * 128],
                            in0=hTD[:, dc * 128:(dc + 1) * 128],
                            in1=psI[:, 0:128], op=ALU.mult))
                        psI_free[pb] = t_x
                    hTD_free = t_x
                phase_barrier(ctx)
            with ExitStack() as st2:
                c2 = _subctx(ctx, st2)
                WT = c2.sb("D_WT", [128, NCH * 512], BF16)
                wbf = [c2.sb(f"D_w{i}", [128, D], BF16) for i in range(3)]
                ysb = [c2.sb("D_y0", [128, 512], F32), c2.sb("D_y1", [128, 512], F32)]
                wt_free = None
                ev_free = [None, None]
                st_free = [None, None]
                wb_free = [None, None, None]
                sg_rdy = {}
                wb_of = {}
                slot_ctr = [0]

                def prep_chunk(oc_, sc_):
                    wb = slot_ctr[0] % 3
                    slot_ctr[0] += 1
                    row0 = oc_ * 512 + sc_ * 128
                    WD.wait(wb_free[wb])
                    tw = WD.mark(nc.gpsimd.dma_start(out=wbf[wb][:], in_=dw[row0:row0 + 128, :]))
                    A_.wait(tw, wb_free[wb])
                    sg_rdy[(oc_, sc_)] = A_.mark(nc.scalar.activation(
                        out=wbf[wb][:], in_=wbf[wb][:], func=AF.Sign))
                    wb_of[(oc_, sc_)] = wb

                for sc_ in range(3):
                    prep_chunk(0, sc_)
                for oc in range(8):
                    tr_tok = None
                    for sc_ in range(4):
                        wb = wb_of[(oc, sc_)]
                        tr_tok = emit_tr32(ctx, lambda ic, wb=wb: wbf[wb][:, ic * 128:(ic + 1) * 128],
                                           WT, lambda ic, sc_=sc_: ic * 512 + sc_ * 128,
                                           [sg_rdy[(oc, sc_)], wt_free], n_chunks=32)
                        wb_free[wb] = tr_tok
                        if sc_ == 0:
                            prep_chunk(oc, 3)
                    last_mm = None
                    for tc in range(NT):
                        if oc + 1 < 8 and tc in (8, 11, 14):
                            prep_chunk(oc + 1, (tc - 8) // 3)
                        pb = tc % 2
                        ps = (B0, B1)[pb]
                        T.wait(tr_tok, ev_free[pb])
                        ins = None
                        for dc in range(NCH):
                            ins = nc.tensor.matmul(ps[:], XST[:, dc * S + tc * 128: dc * S + (tc + 1) * 128],
                                                   WT[:, dc * 512:(dc + 1) * 512],
                                                   start=(dc == 0), stop=(dc == NCH - 1))
                        tmm = T.mark(ins)
                        last_mm = tmm
                        A_.wait(tmm, st_free[pb])
                        tev = A_.mark(nc.scalar.activation(out=ysb[pb][:], in_=ps[:], func=AF.Copy, scale=1.0))
                        ev_free[pb] = tev
                        SD.wait(tev, *rd_guard)
                        st_free[pb] = SD.mark(nc.scalar.dma_start(
                            out=y_dram[tc * 128:(tc + 1) * 128, oc * 512:(oc + 1) * 512], in_=ysb[pb][:]))
                    wt_free = last_mm
                phase_barrier(ctx)
            y_ready = ctx.SD.tok()
        with ExitStack() as st3:
            c3 = _subctx(ctx, st3)
            yt = [c3.sb("De_y0", [128, D], F32), c3.sb("De_y1", [128, D], F32)]
            xt = [c3.sb("De_x0", [128, D], F32), c3.sb("De_x1", [128, D], F32)]
            trash = c3.sb("De_tr", [128, D], BF16)
            biasB = c3.sb("De_bias", [128, D], F32)
            ocsb = c3.sb("De_ocs", [8, D], F32)
            dbb = c3.sb("De_db", [1, D], BF16)
            tmE = [_ln_smalls(c3, "De0"), _ln_smalls(c3, "De1")]
            t_oc = ctx.LD.mark(nc.sync.dma_start(out=ocsb[0:8, :], in_=docs))
            t_db = WD.mark(nc.gpsimd.dma_start(out=dbb[0:1, :], in_=dbias.rearrange("(o n) -> o n", o=1)))
            bias_tok = None
            for n8 in range(8):
                T.wait(t_db, bias_tok)
                t_bb = T.mark(nc.tensor.matmul(B5[:], ctx.ones1_bf[:], dbb[0:1, n8 * 512:(n8 + 1) * 512],
                                               start=True, stop=True))
                A_.wait(t_bb)
                bias_tok = A_.mark(nc.scalar.activation(out=biasB[:, n8 * 512:(n8 + 1) * 512],
                                                        in_=B5[:], func=AF.Copy, scale=1.0))
            yfree = [None, None]
            xfree = [None, None]
            psO_free = [None, None]
            for tc in range(NT):
                b = tc % 2
                rows = slice(tc * 128, (tc + 1) * 128)
                LD.wait(yfree[b], y_ready)
                ty = LD.mark(nc.sync.dma_start(out=yt[b][:], in_=y_dram[rows, :]))
                LD.wait(xfree[b])
                tx = LD.mark(nc.sync.dma_start(out=xt[b][:], in_=x_src[1][rows, :]))
                mul_toks = []
                for n8 in range(8):
                    pb = n8 % 2
                    psO = (B2, B3)[pb]
                    cols = slice(n8 * 512, (n8 + 1) * 512)
                    T.wait(psO_free[pb], t_oc)
                    t_o = T.mark(nc.tensor.matmul(psO[:], routT[0:8, tc * 128:(tc + 1) * 128],
                                                  ocsb[0:8, n8 * 512:(n8 + 1) * 512],
                                                  start=True, stop=True))
                    V.wait(t_o, ty, bias_tok)
                    t_m = V.mark(nc.vector.tensor_tensor(out=yt[b][:, cols], in0=yt[b][:, cols],
                                                         in1=psO[:], op=ALU.mult))
                    psO_free[pb] = t_m
                    V.wait(t_m)
                    t_m = V.mark(nc.vector.tensor_tensor(out=yt[b][:, cols], in0=yt[b][:, cols],
                                                         in1=biasB[:, cols], op=ALU.add))
                    mul_toks.append(t_m)
                t_ba = mul_toks[-1]
                t_ln = emit_ln(ctx, yt[b][:], trash[:], tmE[b], [yt[b][:]], [t_ba],
                               out_deps=(), trash_dep=())[0]
                G.wait(t_ln, tx)
                t_add = G.mark(nc.gpsimd.tensor_tensor(out=yt[b][:], in0=yt[b][:], in1=xt[b][:], op=ALU.add))
                xfree[b] = t_add
                SD.wait(t_add)
                yfree[b] = SD.mark(nc.scalar.dma_start(out=out_dram[rows, :], in_=yt[b][:]))
        phase_barrier(ctx)
    return ctx.SD.tok()



def build_program(upto="D", dbg=False):
    _NAME_CTR[0] = 0
    nc = bass.Bass("TRN2", target_bir_lowering=False)
    x_in = nc.dram_tensor("x", [S, D], F32, kind="ExternalInput").ap()
    wq = nc.dram_tensor("wq", [D, D], F32, kind="ExternalInput").ap()
    wk = nc.dram_tensor("wk", [D, D], F32, kind="ExternalInput").ap()
    wv = nc.dram_tensor("wv", [D, D], F32, kind="ExternalInput").ap()
    wo = nc.dram_tensor("wo", [D, D], F32, kind="ExternalInput").ap()
    fq = nc.dram_tensor("fq", [D, D], I32, kind="ExternalInput").ap()
    fsc = nc.dram_tensor("fsc", [D, 256], F32, kind="ExternalInput").ap()
    cm = nc.dram_tensor("cm", [D, D], I32, kind="ExternalInput").ap()
    cw = nc.dram_tensor("cw", [D, D], F32, kind="ExternalInput").ap()
    cb = nc.dram_tensor("cb", [D], F32, kind="ExternalInput").ap()
    dw = nc.dram_tensor("dw", [D, D], F32, kind="ExternalInput").ap()
    db = nc.dram_tensor("db", [D], F32, kind="ExternalInput").ap()
    dgw = nc.dram_tensor("dgw", [8, D], F32, kind="ExternalInput").ap()
    dics = nc.dram_tensor("dics", [8, D], F32, kind="ExternalInput").ap()
    docs = nc.dram_tensor("docs", [8, D], F32, kind="ExternalInput").ap()
    out = nc.dram_tensor("out", [S, D], F32, kind="ExternalOutput").ap()

    kd = "ExternalOutput" if dbg else "Internal"
    xB = nc.dram_tensor("xB", [S, D], F32, kind=kd).ap()
    xC = nc.dram_tensor("xC", [S, D], F32, kind=kd).ap()
    yF = nc.dram_tensor("yF", [S, D], F32, kind=kd).ap()
    yB = nc.dram_tensor("yBd", [S, D], BF16, kind=kd).ap()
    wqd = nc.dram_tensor("wqd", [D, D], BF16, kind=kd).ap()
    wo_bf = nc.dram_tensor("wo_bf", [D, D], BF16, kind="Internal").ap()
    at = nc.dram_tensor("atd", [D, S], BF16, kind=kd).ap()

    with ExitStack() as stack:
        ctx = Box()
        ctx.nc = nc
        ctx.sb = lambda name, shape, dt: stack.enter_context(nc.sbuf_tensor(name, shape, dt))
        ctx.T = Qw(stack, nc, nc.tensor, "sPE")
        ctx.V = Qw(stack, nc, nc.vector, "sV")
        ctx.A = Qw(stack, nc, nc.scalar, "sA")
        ctx.G = Qw(stack, nc, nc.gpsimd, "sG")
        ctx.LD = Qw(stack, nc, nc.sync, "sLD", dma=True)
        ctx.WD = Qw(stack, nc, nc.gpsimd, "sWD", dma=True)
        ctx.SD = Qw(stack, nc, nc.scalar, "sSD", dma=True)

        ctx.PB = [stack.enter_context(nc.psum_tensor(f"PB{i}", [128, 512], F32)) for i in range(6)]
        ctx.T0 = stack.enter_context(nc.psum_tensor("PT0", [128, 1024], BF16))
        ctx.T1 = stack.enter_context(nc.psum_tensor("PT1", [128, 1024], BF16))
        ctx.tbank_free = [None, None]

        ctx.ident = ctx.sb("ident", [128, 128], BF16)
        ctx.tri01 = ctx.sb("tri01", [128, 128], BF16)
        ctx.ones1_bf = ctx.sb("ones1b", [1, 128], BF16)
        ctx.ones1_f32 = ctx.sb("ones1f", [1, 128], F32)
        ctx.ones128_bf = ctx.sb("ones128b", [128, 128], BF16)
        ctx.blk4_bf = ctx.sb("blk4", [128, 4], BF16)
        ctx.e4_bf = ctx.sb("e4", [4, 128], BF16)
        ctx.eps128 = ctx.sb("eps128", [128, 1], F32)
        ctx.cm2 = ctx.sb("cm2", [128, 1], F32)
        ctx.cm6 = ctx.sb("cm6", [128, 1], F32)

        emit_const(ctx, "ident", np.eye(128, dtype=BF), ctx.ident)
        emit_const(ctx, "tri01", _tri01(), ctx.tri01)
        blk4 = np.zeros((128, 4), dtype=BF)
        for p in range(128):
            blk4[p, p // 32] = 1.0
        e4 = np.zeros((4, 128), dtype=BF)
        for p in range(128):
            e4[p // 32, p] = 1.0
        emit_const(ctx, "blk4", blk4, ctx.blk4_bf)
        tok = emit_const(ctx, "e4", e4, ctx.e4_bf)
        ctx.V.wait(tok)
        for tile, val in ((ctx.ones1_bf, 1.0), (ctx.ones1_f32, 1.0), (ctx.ones128_bf, 1.0),
                          (ctx.eps128, EPS), (ctx.cm2, -2.0), (ctx.cm6, -6.0)):
            ctx.V.mark(nc.vector.memset(tile[:], val))

        adbg = None
        if dbg and upto == "att":
            adbg = {"esc0": nc.dram_tensor("d_esc0", [128, 512], F32, kind="ExternalOutput").ap(),
                    "pt0": nc.dram_tensor("d_pt0", [128, 512], BF16, kind="ExternalOutput").ap(),
                    "rcb0": nc.dram_tensor("d_rcb0", [128, 512], F32, kind="ExternalOutput").ap()}
        if dbg and upto in ("A0", "qk"):
            adbg = {"stop": upto,
                    "hT": nc.dram_tensor("d_hT", [128, NCH * S], BF16, kind="ExternalOutput").ap(),
                    "QT": nc.dram_tensor("d_QT", [128, S], BF16, kind="ExternalOutput").ap(),
                    "KT": nc.dram_tensor("d_KT", [128, S], BF16, kind="ExternalOutput").ap(),
                    "Vtm": nc.dram_tensor("d_Vtm", [128, S], BF16, kind="ExternalOutput").ap(),
                    "sm": nc.dram_tensor("d_sm", [128, 64], F32, kind="ExternalOutput").ap()}
        with nc.named_scope("phaseA"):
            emit_phaseA(ctx, x_in, wq, wk, wv, at, adbg=adbg, wo=wo, wo_bf=wo_bf)
        if upto == "att":
            upto = "A"
        if upto in ("A0", "qk"):
            nc.sync.wait_ge(ctx.SD.sem, ctx.SD.n)
            nc.scalar.wait_ge(ctx.SD.sem, ctx.SD.n)
            return nc
        at_tok = ctx.SD.tok()
        if upto != "A":
            with nc.named_scope("wo"):
                emit_wo(ctx, at, wo_bf, yF, at_tok, fq, fsc, wqd)
        if upto not in ("A", "wo"):
            with nc.named_scope("phaseB"):
                emit_phaseB(ctx, (x_in, xB), yF, wqd, yB)
        if upto not in ("A", "wo", "B"):
            with nc.named_scope("phaseC"):
                emit_phaseC(ctx, (xB, xC), yB, cm, cw, cb, yF)
        if upto not in ("A", "wo", "B", "C"):
            with nc.named_scope("phaseD"):
                emit_phaseD(ctx, (xC, xB), yF, dw, db, dgw, dics, docs, yF, out)

        nc.sync.wait_ge(ctx.SD.sem, ctx.SD.n)
        nc.scalar.wait_ge(ctx.SD.sem, ctx.SD.n)
    return nc


_PROG_CACHE = {}


def kernel(**inputs):
    for _dead, _shape in (("L0_attn_kq1", (32, 128, 128)), ("L0_attn_kq2", (32, 128, 64)),
                          ("L0_attn_kk1", (32, 128, 128)), ("L0_attn_kk2", (32, 128, 64)),
                          ("L0_attn_scalingD", (1, 32, 1, 64)), ("L0_attn_scalingD2", (1, 32, 1, 64)),
                          ("L0_attn_interaction_k", (32, 64, 64))):
        assert tuple(np.asarray(inputs[_dead]).shape) == _shape
    x = np.ascontiguousarray(np.asarray(inputs["L0_attn_hidden_states"], dtype=np.float32))
    B = x.shape[0]
    base = {
        "wq": np.ascontiguousarray(np.asarray(inputs["L0_attn_wq"], dtype=np.float32)),
        "wk": np.ascontiguousarray(np.asarray(inputs["L0_attn_wk"], dtype=np.float32)),
        "wv": np.ascontiguousarray(np.asarray(inputs["L0_attn_wv"], dtype=np.float32)),
        "wo": np.ascontiguousarray(np.asarray(inputs["L0_attn_wo"], dtype=np.float32)),
        "fq": np.ascontiguousarray(np.asarray(inputs["L0_ffn_weight_q"], dtype=np.int32)),
        "fsc": np.ascontiguousarray(np.asarray(inputs["L0_ffn_weight_scales"], dtype=np.float32)),
        "cm": np.ascontiguousarray(np.asarray(inputs["L1_attn_mask"], dtype=np.int32)),
        "cw": np.ascontiguousarray(np.asarray(inputs["L1_attn_weight"], dtype=np.float32)),
        "cb": np.ascontiguousarray(np.asarray(inputs["L1_attn_bias"], dtype=np.float32)),
        "dw": np.ascontiguousarray(np.asarray(inputs["L1_ffn_weight"], dtype=np.float32)),
        "db": np.ascontiguousarray(np.asarray(inputs["L1_ffn_bias"], dtype=np.float32)),
        "dgw": np.ascontiguousarray(np.asarray(inputs["L1_ffn_gate_w"], dtype=np.float32)),
        "dics": np.ascontiguousarray(np.asarray(inputs["L1_ffn_in_channel_scale"], dtype=np.float32)),
        "docs": np.ascontiguousarray(np.asarray(inputs["L1_ffn_out_channel_scale"], dtype=np.float32)),
    }
    if "prog" not in _PROG_CACHE:
        _PROG_CACHE["prog"] = build_program()
    nc = _PROG_CACHE["prog"]
    in_maps = []
    for b in range(B):
        m = dict(base)
        m["x"] = np.ascontiguousarray(x[b])
        in_maps.append(m)
    res = run_bass_kernel_spmd(nc, in_maps, core_ids=list(range(B)))
    return np.stack([r["out"] for r in res.results], axis=0)
